# version 09 (final)
# baseline (speedup 1.0000x reference)
import math
import numpy as np
import ml_dtypes
import concourse.bass as bass
import concourse.mybir as mybir
from concourse.bass_utils import run_bass_kernel_spmd
from concourse.bass import IndirectOffsetOnAxis

F32 = mybir.dt.float32
BF16 = mybir.dt.bfloat16
I32 = mybir.dt.int32
AF = mybir.ActivationFunctionType
ALU = mybir.AluOpType
AX = mybir.AxisListType

N = 4096
D = 1024
H = 8
DH = 64
NSEL = 1024
LOGK = math.log(1024 * 9.0 / 8.0)
NEG = -3.4028235e38
MNEG = -1.0e38


_CONST = None

def host_constants():
    global _CONST
    if _CONST is not None:
        return _CONST
    c = {}
    T = np.arange(N, dtype=np.int64)
    tp = ((T % 128) * 32 + T // 128).astype(np.int64)
    cpt = np.zeros((128, 64), np.float32)
    cpt[:, 0::2] = T.reshape(32, 128).T
    cpt[:, 1::2] = tp.reshape(32, 128).T
    c["cpairs_tile"] = cpt
    tri = (np.arange(128)[:, None] <= np.arange(128)[None, :])
    c["trib"] = tri.astype(ml_dtypes.bfloat16)
    c["iota128"] = np.broadcast_to(np.arange(128, dtype=np.float32), (128, 128)).copy()

    inv_freq = (1.0 / 10000.0 ** (np.arange(0, DH, 2, dtype=np.float32) / DH)).astype(np.float32)
    freqs = (T.astype(np.float32)[:, None] * inv_freq[None]).astype(np.float32)
    pos = np.concatenate([freqs, freqs], 1)
    c["hrope"] = np.concatenate([np.cos(pos), np.sin(pos)], 1).astype(np.float32)

    W = 512
    half = np.arange(0, DH, 2, dtype=np.float32)
    ivf = (1.0 / 10000.0 ** (half / DH)).astype(np.float32)
    fr = T.astype(np.float32)[:, None] * ivf[None]
    fr = np.concatenate([fr, fr], 1).astype(np.float32)
    base = ((half + 0.4 * DH) / (1.4 * DH)).astype(np.float32)
    power = ((T.astype(np.float32) - N // 2) / float(W // 2)).astype(np.float32)
    scale = (base[None, :] ** power[:, None]).astype(np.float32)
    scale = np.concatenate([scale, scale], 1).astype(np.float32)
    cos = np.cos(fr).astype(np.float32); sin = np.sin(fr).astype(np.float32)
    c["frope"] = np.concatenate([cos * scale, sin * scale, cos / scale, sin / scale], 1).astype(np.float32)

    c["identb"] = np.eye(128, dtype=ml_dtypes.bfloat16)
    c["identf"] = np.eye(128, dtype=np.float32)
    c["ones_1x128"] = np.ones((1, 128), np.float32)
    c["ones_128x1"] = np.ones((128, 1), np.float32)
    c["onesb_1x128"] = np.ones((1, 128), ml_dtypes.bfloat16)
    c["onesb_128x1"] = np.ones((128, 1), ml_dtypes.bfloat16)

    def lmask(t0):
        kw = min(max(t0 - 128, 0), N - 384)
        i = t0 + np.arange(128)[:, None]
        j = kw + np.arange(384)[None, :]
        return np.where(np.abs(i - j) <= 64, 0.0, MNEG).astype(np.float32)
    c["lmask"] = np.concatenate([lmask(0), lmask(128), lmask(N - 128)], 1).astype(ml_dtypes.bfloat16)

    p = np.arange(128)[:, None]
    cx = np.arange(1152)[None, :]
    c["fmaskX"] = np.where((cx < p) | (cx > p + 512), MNEG, 0.0).astype(ml_dtypes.bfloat16)
    d_ = np.arange(128)
    c["bdones2"] = (np.stack([(d_ // 64) == 0, (d_ // 64) == 1], 1)).astype(ml_dtypes.bfloat16)
    sel2 = np.zeros((66, 128), np.float32)
    sel2[0] = sel2[64] = (d_ // 64) == 0
    sel2[1] = sel2[65] = (d_ // 64) == 1
    c["sel2"] = sel2.astype(ml_dtypes.bfloat16)
    _CONST = c
    return c



ENGINES = ("sync", "gpsimd", "vector", "scalar", "tensor")

RACE_DETECT = False

class KB:
    def __init__(self, debug=()):
        self.nc = bass.Bass("TRN2", target_bir_lowering=False,
                            detect_race_conditions=RACE_DETECT)
        self.debug = set(debug)
        self.q = {e: [] for e in ENGINES}
        self.cnt = {e: 0 for e in ENGINES}
        self.chan = {}
        self.taps = {}
        self.final_stores = []
        self.SB = {}
        self.csem = {e: self.nc.alloc_semaphore("chain_" + e)
                     for e in ("vector", "scalar", "gpsimd")}
        self.ccnt = {e: 0 for e in ("vector", "scalar", "gpsimd")}

    def tks(self, e, k):
        self.ccnt[e] += k
        return list(range(self.ccnt[e] - k + 1, self.ccnt[e] + 1))

    def sb(self, name, shape, dt=F32):
        t = self.nc.alloc_sbuf_tensor(name, list(shape), dt)
        self.SB[name] = t
        return t

    def psum(self, name, shape, dt=F32):
        return self.nc.alloc_psum_tensor(name, list(shape), dt)

    def on(self, e, fn, waits=(), inc=0):
        self.q[e].append((fn, tuple(waits), inc))
        if inc:
            self.cnt[e] += inc
        return (e, self.cnt[e])

    def tok(self, e):
        return (e, self.cnt[e])

    def fence(self, e, waits=()):
        fb = self.fb
        col = {"vector": 0, "scalar": 1, "gpsimd": 2}[e]
        cflush = self.ccnt[e]
        def fn(eng, e=e, col=col, cflush=cflush):
            eng.wait_ge(self.csem[e], cflush)
            if e == "scalar":
                return eng.memzero(fb[:1, col:col + 1])
            return eng.memset(fb[:1, col:col + 1], 0.0)
        return self.on(e, fn, waits=waits, inc=1)

    def dma(self, e, ch, out, in_, waits=(), **kw):
        self.chan[ch] = self.chan.get(ch, 0) + 16
        val = self.chan[ch]
        def fn(eng, out=out, in_=in_, kw=kw):
            return eng.dma_start(out=out, in_=in_, **kw)
        self.q[e].append((fn, tuple(waits), ("dma", ch)))
        return ("ch", ch, val)

    def idma(self, ch, out, out_off, in_, in_off, waits=(), **kw):
        self.chan[ch] = self.chan.get(ch, 0) + 16
        val = self.chan[ch]
        def fn(eng, out=out, in_=in_, kw=kw):
            return eng.indirect_dma_start(out=out, out_offset=out_off, in_=in_,
                                          in_offset=in_off, **kw)
        self.q["gpsimd"].append((fn, tuple(waits), ("dma", ch)))
        return ("ch", ch, val)

    def chfinal(self, ch):
        return ("chF", ch, 0)

    def tap(self, name, src_ap, shape, dt=F32):
        if name in self.debug:
            t = self.nc.dram_tensor("tap_" + name, list(shape), dt, kind="ExternalOutput").ap()
            self.taps[name] = t
            self.final_stores.append((t, src_ap))

    def store(self, out_ap, src_ap):
        self.final_stores.append((out_ap, src_ap))

    def realize(self):
        nc = self.nc
        semmap = {e: nc.alloc_semaphore("sem_" + e) for e in ENGINES}
        chsem = {ch: nc.alloc_semaphore("ch_" + ch) for ch in self.chan}
        stsem = nc.alloc_semaphore("stores")
        finals = dict(self.cnt)
        chfin = dict(self.chan)

        def wait_tok(eng, t):
            we = t[0]
            if we == "ch":
                eng.wait_ge(chsem[t[1]], t[2])
            elif we == "chF":
                eng.wait_ge(chsem[t[1]], chfin[t[1]])
            elif we == "chain_v":
                eng.wait_ge(self.csem["vector"], t[1])
            elif we == "chain_s":
                eng.wait_ge(self.csem["scalar"], t[1])
            elif we == "chain_g":
                eng.wait_ge(self.csem["gpsimd"], t[1])
            else:
                eng.wait_ge(semmap[we], t[1])

        def runner(e):
            def _run(eng):
                for qi, (fn, waits, inc) in enumerate(self.q[e]):
                    for t in waits:
                        wait_tok(eng, t)
                    try:
                        inst = fn(eng)
                    except Exception as ex:
                        raise RuntimeError(f"emit fail engine={e} qi={qi} fn={fn}") from ex
                    if inc == 0:
                        pass
                    elif isinstance(inc, tuple):
                        inst.then_inc(chsem[inc[1]], 16)
                    else:
                        inst.then_inc(semmap[e], inc)
                if e == "sync":
                    for oe in ENGINES:
                        if finals[oe]:
                            eng.wait_ge(semmap[oe], finals[oe])
                    for ch, v in chfin.items():
                        eng.wait_ge(chsem[ch], v)
                    nst = 0
                    for (dst, src) in self.final_stores:
                        eng.dma_start(out=dst, in_=src).then_inc(stsem, 16)
                        nst += 16
                    eng.wait_ge(stsem, nst)
            return _run

        with nc.Block() as blk:
            blk.sync(runner("sync"))
            blk.gpsimd(runner("gpsimd"))
            blk.vector(runner("vector"))
            blk.scalar(runner("scalar"))
            blk.tensor(runner("tensor"))
        return nc



def build_program(debug=(), ffn_stop=None):
    C = host_constants()
    kb = KB(debug)
    nc = kb.nc

    din = lambda name, shape, dt=F32: nc.dram_tensor(name, list(shape), dt, kind="ExternalInput").ap()
    x_h = din("x", (N, D))
    w = {}
    wspec = {
        "ln_gamma": (D,), "ln_beta": (D,),
        "light_qkv_w": (D, 1536), "light_out_w": (512, D),
        "heavy_gamma": (D,), "heavy_q_w": (D, 512), "heavy_kv_w": (H, 128, D),
        "heavy_null_kv": (2, H, DH), "heavy_out_w": (512, D), "null_q_token": (D,),
        "q_route_tok": (D,), "kv_route_tok": (D,),
        "f_wqkv": (2, D, 1536), "f_qs": (2, DH), "f_ks": (2, DH),
        "f_wo": (2, 512, D), "f_lg": (2, D), "f_lb": (2, D),
        "f_w1p": (2, 22, 128, 2048), "f_b1p": (2, 22, 256),
        "f_w2": (2, 2730, D), "f_b2": (2, D),
    }
    for k, s in wspec.items():
        w[k] = din("w_" + k, s)
    cst = {}
    for k, v in C.items():
        dt = {np.dtype(np.float32): F32, np.dtype(np.int32): I32,
              np.dtype(ml_dtypes.bfloat16): BF16}[v.dtype]
        cst[k] = din("c_" + k, v.shape, dt)
    out_h = nc.dram_tensor("out", [N, D], F32, kind="ExternalOutput").ap()

    d_pairs = {r: nc.dram_tensor(f"d_pairs_{r}", [4224, 2], F32, kind="Internal").ap()
               for r in ("q", "k")}
    d_xspill = nc.dram_tensor("d_xspill", [N, D], F32, kind="Internal").ap()
    d_aout = nc.dram_tensor("d_aout", [N, D], F32, kind="Internal").ap()

    sb = kb.sb
    kb.fb = sb("fencebuf", [1, 8], F32)

    identb = sb("identb", [128, 128], BF16)
    identf = sb("identf", [128, 128], F32)
    ones1x = sb("ones1x", [1, 128], F32)
    onesb1x = sb("onesb1x", [1, 128], BF16)
    onescol = sb("onescol", [128, 1], F32)
    onesbcol = sb("onesbcol", [128, 1], BF16)
    null_bc = sb("null_bc", [128, D], F32)
    lmasks = sb("lmasks", [128, 1152], BF16)
    maskX_sb = sb("maskX_sb", [128, 1152], BF16)
    bdones_sb = sb("bdones_sb", [128, 2], BF16)
    sel2_sb = sb("sel2_sb", [66, 128], BF16)
    lgb = sb("lgb", [128, 24], F32)
    qkf = sb("qkf", [128, 4], F32)
    b1c = sb("b1c", [128, 2], F32)
    lbb = sb("lbb", [128, 8], BF16)
    trib_sb = sb("trib_sb", [128, 128], BF16)
    iota_sb = sb("iota_sb", [128, 128], F32)
    cpt_sb = sb("cpt_sb", [128, 64], F32)

    CUR = sb("CUR", [128, 32 * D], F32)
    Abf = CUR[:].bitcast(BF16)
    GST = [CUR[:, 0:4096], CUR[:, 4096:8192]]
    t_spill_prev = {}
    SCR2H = CUR[:, 16384:]
    def ob(c0, c1):
        return Abf[:, 32768 + c0:32768 + c1]
    WREG = sb("WREG", [128, 6208], F32)
    WREGb = WREG[:].bitcast(BF16)

    xs = [sb(f"xtile{i}", [128, D], F32) for i in range(2)]
    act_trash = sb("act_trash", [128, D], F32)
    stat = sb("stat", [128, 16], F32)
    part8 = sb("part8", [128, 16], F32)
    s_sb = sb("s_sb", [128, 64], F32)
    tokT = sb("tokT", [128, 2], F32)
    ccol = sb("ccol", [128, 2], F32)
    u2n_t = [sb(f"u2nt{i}", [128, D], BF16) for i in range(2)]
    tok_bc = sb("tok_bc", [128, 2 * D], F32)

    relu_t = sb("relu_t", [128, 64], F32)
    sbp = sb("sbp", [128, 64], F32)
    expb = sb("expb", [128, 64], F32)
    esum = sb("esum", [128, 2], F32)
    acur = sb("acur", [128, 4], F32)
    abrow = sb("abrow", [1, 8], F32)
    smrow = sb("smrow", [1, 2], F32)
    tot2 = sb("tot2", [128, 2], F32)
    scr2 = sb("scr2", [2, 128], F32)
    scr2b = sb("scr2b", [2, 4], F32)
    mask_t = sb("mask_t", [128, 64], F32)
    maskb = sb("maskb", [128, 64], BF16)
    mrank = sb("mrank", [128, 64], F32)
    mrank_m1 = sb("mrank_m1", [128, 64], F32)
    Ct = [sb(f"Ct{i}", [128, 128], F32) for i in range(2)]
    pairs_f = {r: sb(f"pairs_f_{r}", [128, 64], F32) for r in ("q", "k")}
    basef = sb("basef", [1, 64], F32)
    base_i = sb("base_i", [1, 64], I32)
    srow = sb("srow", [1, 64], F32)
    idxs = {r: sb(f"idxs_{r}", [128, 16], I32) for r in ("q", "k")}
    idxf = sb("idxf", [128, 32], F32)
    gam = sb("gam", [128, 32], F32)
    bqkv = sb("bqkv", [128, 12], F32)
    wtmp = sb("wtmp", [128, 1536], F32)
    bt8 = sb("bt8", [8, 128], F32)
    btmpb = sb("btmpb", [128, 8], BF16)

    u2T = [ob(0, 4096), ob(4096, 8192)]
    RING = 10
    kring = [ob(8192 + 512 * i, 8192 + 512 * (i + 1)) for i in range(RING)]
    vring = [ob(13312 + 512 * i, 13312 + 512 * (i + 1)) for i in range(RING)]
    qbuf = [ob(18432 + 2048 * i, 18432 + 2048 * (i + 1)) for i in range(2)]
    attno = ob(22528, 24576)
    PT_sb = ob(24576, 24960)
    vtmpb = ob(24960, 25472)
    Wout_sb = ob(25472, 29568)
    Pbb = ob(29568, 29952)
    P_u = CUR[:, 31360:31744]
    scol = CUR[:, 31744:31760]
    loev = CUR[:, 31760:32272]
    Wqkv_sb = WREGb[:, 0:12288]

    PSALL = kb.psum("psall", [128, 4096], F32)[:]
    PSBALL = PSALL.bitcast(BF16)
    PB = [PSALL[:, 512 * i:512 * (i + 1)] for i in range(6)]
    PBB = [PSBALL[:, 6144 + 1024 * i:6144 + 1024 * (i + 1)] for i in range(2)]
    S_H = PSALL[:, 1024:2049]
    PV_H = PSALL[0:64, 2432:2560]
    ps_tr = PB[0][:, 0:128]
    ps_row = PB[1][0:1, 0:16]
    ps_bc = PB[1][:, 16:32]
    ps_rank = PB[2][:, 0:64]
    ps_tot = PB[1][0:1, 32:96]
    ps_pair = PB[2][:, 64:128]

    on, dma, idma, fence, tok = kb.on, kb.dma, kb.idma, kb.fence, kb.tok

    t_if_unused = dma("sync", "const", identf[:], cst["identf"])
    dma("sync", "const", identb[:], cst["identb"])
    dma("sync", "const", ones1x[:], cst["ones_1x128"])
    dma("sync", "const", onesb1x[:], cst["onesb_1x128"])
    dma("sync", "const", onescol[:], cst["ones_128x1"])
    dma("sync", "const", null_bc[:], w["null_q_token"].unsqueeze(0).partition_broadcast(128))
    dma("sync", "const", tok_bc[:, :D], w["q_route_tok"].unsqueeze(0).partition_broadcast(128))
    t_tokld_unused = dma("sync", "const", tok_bc[:, D:], w["kv_route_tok"].unsqueeze(0).partition_broadcast(128))
    t_lm_unused = dma("sync", "const", lmasks[:], cst["lmask"])
    t_tri_unused = dma("sync", "const", trib_sb[:], cst["trib"])
    t_iota_unused = dma("sync", "const", iota_sb[:], cst["iota128"])
    t_cpt_unused = dma("sync", "const", cpt_sb[:], cst["cpairs_tile"])
    t_ob_unused = dma("sync", "const", onesbcol[:], cst["onesb_128x1"])
    dma("sync", "const", maskX_sb[:], cst["fmaskX"])
    dma("sync", "const", bdones_sb[:], cst["bdones2"])
    dma("sync", "const", sel2_sb[:], cst["sel2"])

    TCONST = kb.chfinal("const")
    t_if = t_tokld = t_lm = t_tri = t_iota = t_cpt = t_ob = TCONST
    CV, CS, CG = kb.csem["vector"], kb.csem["scalar"], kb.csem["gpsimd"]
    CV0 = CV
    vr0 = kb.tks("vector", 2)
    def pre_fn(v):
        v.memset(ccol[:, 0:1], 1e-5).then_inc(CV0, 1)
        return v.reduce_sum(
            out=tokT[:].rearrange("p (r o) -> p r o", r=2, o=1),
            in_=tok_bc[:].rearrange("p (r d) -> p r d", r=2),
            axis=AX.X).then_inc(CV0, 1)
    on("vector", pre_fn, waits=[t_tokld], inc=0)
    t_tok = fence("vector")

    gvecs = [("ln_gamma", 0), ("ln_beta", 8), ("heavy_gamma", 16)]
    t_gv = {}
    t_prev_tr = None
    for name, c0 in gvecs:
        tl = dma("sync", "wload", bt8[:], w[name].rearrange("(c p) -> c p", p=128),
                 waits=[t_prev_tr] if t_prev_tr else [])
        tt = on("tensor", lambda t, c0=c0: t.transpose(ps_tr[:, c0:c0 + 8], bt8[:8, :],
                                                       identf[:8, :8]),
                waits=[tl, TCONST], inc=1)
        t_prev_tr = tt
        vr = kb.tks("vector", 1)
        on("vector", lambda v, c0=c0: v.tensor_copy(out=gam[:, c0:c0 + 8],
                                                    in_=ps_tr[:, c0:c0 + 8]).then_inc(CV, 1),
           waits=[tt])
        t_gv[name] = fence("vector")

    vr = kb.tks("vector", 1)
    on("vector", lambda v: v.tensor_copy(out=btmpb[:], in_=gam[:, 8:16]).then_inc(CV, 1),
       waits=[t_gv["ln_beta"]])
    t_btb = fence("vector")

    t_wc = None
    for kc in range(8):
        wl = [t_wc] if t_wc else []
        tl = dma("sync", "wload", wtmp[:], w["light_qkv_w"][kc * 128:(kc + 1) * 128, :],
                 waits=wl)
        vr = kb.tks("vector", 1)
        on("vector", lambda v, kc=kc: v.tensor_copy(
            out=Wqkv_sb[:, kc * 1536:(kc + 1) * 1536], in_=wtmp[:]).then_inc(CV, 1),
            waits=[tl])
        t_wc = fence("vector")
    brow = wtmp[0:1, 1024:1536]
    for cg in range(3):
        for kc in range(8):
            on("tensor", lambda t, cg=cg, kc=kc: t.matmul(
                PB[3][0:1, 0:512], btmpb[:, kc:kc + 1],
                Wqkv_sb[:, kc * 1536 + cg * 512:kc * 1536 + (cg + 1) * 512],
                start=(kc == 0), stop=(kc == 7)),
               waits=[t_wc, t_btb] if kc == 0 else [], inc=(1 if kc == 7 else 0))
        tmm = kb.tok("tensor")
        vr = kb.tks("vector", 1)
        on("vector", lambda v: v.tensor_copy(out=brow, in_=PB[3][0:1, 0:512]).then_inc(CV, 1),
           waits=[tmm])
        tbr = fence("vector")
        for j in range(4):
            on("tensor", lambda t, j=j: t.transpose(
                ps_tr[:, 16 + j:17 + j], brow[:, j * 128:(j + 1) * 128],
                identf[0:1, 0:1]), waits=[tbr] if j == 0 else [], inc=(1 if j == 3 else 0))
        vr = kb.tks("vector", 1)
        on("vector", lambda v, cg=cg: v.tensor_copy(
            out=bqkv[:, cg * 4:cg * 4 + 4], in_=ps_tr[:, 16:20]).then_inc(CV, 1),
            waits=[kb.tok("tensor")])
    t_bq = fence("vector")
    vrq = kb.tks("vector", 2)
    def qfold(v):
        v.tensor_scalar_mul(gam[:, 24:32], gam[:, 0:8],
                            0.125).then_inc(CV, 1)
        return v.tensor_scalar_mul(bqkv[:, 0:4], bqkv[:, 0:4],
                                   0.125).then_inc(CV, 1)
    on("vector", qfold, waits=[t_bq, t_gv["ln_gamma"]])
    vr = kb.tks("vector", 16)
    def wscale(v, vrq=vrq):
        last = None
        for kc in range(8):
            last = v.tensor_scalar_mul(Wqkv_sb[:, kc * 1536:kc * 1536 + 512],
                                       Wqkv_sb[:, kc * 1536:kc * 1536 + 512],
                                       gam[:, 24 + kc:25 + kc]).then_inc(CV, 1)
            last = v.tensor_scalar_mul(Wqkv_sb[:, kc * 1536 + 512:(kc + 1) * 1536],
                                       Wqkv_sb[:, kc * 1536 + 512:(kc + 1) * 1536],
                                       gam[:, kc:kc + 1]).then_inc(CV, 1)
        return last
    on("vector", wscale, waits=[("chain_v", vrq[1])])
    t_wqkv = fence("vector")
    t_wc = t_wqkv
    for kc in range(4):
        tl = dma("sync", "wload", wtmp[:, 0:1024],
                 w["light_out_w"][kc * 128:(kc + 1) * 128, :], waits=[t_wc])
        vr = kb.tks("vector", 1)
        on("vector", lambda v, kc=kc: v.tensor_copy(
            out=Wout_sb[:, kc * 1024:(kc + 1) * 1024], in_=wtmp[:, 0:1024]).then_inc(CV, 1),
            waits=[tl])
        t_wc = fence("vector")
    t_wout = t_wc

    g_trash = sb("g_trash", [128, D], F32)
    g_trash2 = sb("g_trash2", [128, D], F32)
    prev = {}
    light_state = {"prev_exp2": None}
    lq_done = {}
    t_qkv = {}
    t_att = {}

    def emit_subtile(st):
        buf = xs[st % 2]
        wl = []
        if st >= 2:
            p2 = prev[st - 2]
            wl = [("chain_s", p2["sq"]), ("chain_v", p2["bufred"]),
                  ("chain_g", p2["dots"]), ("chain_s", p2["u2n"])]
        tk = dma("sync", f"xload{st % 2}", buf[:], x_h[st * 128:(st + 1) * 128, :],
                 waits=wl)
        swl = [tk]
        if st >= 1:
            swl.append(("chain_v", prev[st - 1]["sqred"]))
        s1, = kb.tks("scalar", 1)
        on("scalar", lambda s, buf=buf: s.activation(
            out=act_trash[:], in_=buf[:], func=AF.Square).then_inc(CS, 1),
            waits=swl)

        vt = kb.tks("vector", 8)
        def statsA(v, buf=buf, vt=vt):
            v.reduce_sum(out=part8[:, 0:8].rearrange("p (s o) -> p s o", o=1),
                         in_=buf[:].rearrange("p (s c) -> p s c", s=8),
                         axis=AX.X).then_inc(CV, 1)
            v.reduce_sum(out=part8[:, 8:16].rearrange("p (s o) -> p s o", o=1),
                         in_=act_trash[:].rearrange("p (s c) -> p s c", s=8),
                         axis=AX.X).then_inc(CV, 1)
            v.wait_ge(CV, vt[1])
            v.reduce_sum(out=stat[:, 0:1], in_=part8[:, 0:8], axis=AX.X).then_inc(CV, 1)
            v.reduce_sum(out=stat[:, 1:2], in_=part8[:, 8:16], axis=AX.X).then_inc(CV, 1)
            v.wait_ge(CV, vt[3])
            v.tensor_scalar_mul(stat[:, 2:3], stat[:, 0:1], 1.0 / D).then_inc(CV, 1)
            v.tensor_scalar_mul(stat[:, 3:4], stat[:, 1:2], 1.0 / D).then_inc(CV, 1)
            v.wait_ge(CV, vt[5])
            v.tensor_mul(stat[:, 4:5], stat[:, 2:3], stat[:, 2:3]).then_inc(CV, 1)
            v.wait_ge(CV, vt[6])
            return v.tensor_sub(stat[:, 4:5], stat[:, 3:4], stat[:, 4:5]).then_inc(CV, 1)
        on("vector", statsA, waits=[tk, ("chain_s", s1)])

        s2 = kb.tks("scalar", 2)
        def rfn(s, s2=s2):
            s.activation(out=stat[:, 5:6], in_=stat[:, 4:5], func=AF.Ln,
                         bias=ccol[:, 0:1]).then_inc(CS, 1)
            s.wait_ge(CS, s2[0])
            return s.activation(out=stat[:, 6:7], in_=stat[:, 5:6], func=AF.Exp,
                                scale=-0.5).then_inc(CS, 1)
        on("scalar", rfn, waits=[("chain_v", vt[7])])

        gwl = [tk, t_tok]
        if st >= 1:
            gwl.append(("chain_v", prev[st - 1]["dotred"]))
        gt = kb.tks("gpsimd", 2)
        def dots_fn(g, buf=buf):
            g.tensor_mul(g_trash[:], buf[:], tok_bc[:, :D]).then_inc(CG, 1)
            return g.tensor_mul(g_trash2[:], buf[:], tok_bc[:, D:]).then_inc(CG, 1)
        on("gpsimd", dots_fn, waits=gwl)

        vt2 = kb.tks("vector", 12)
        def murfin(v, st=st, vt2=vt2):
            v.tensor_mul(stat[:, 11:12], stat[:, 4:5], stat[:, 6:7]).then_inc(CV, 1)
            v.wait_ge(CV, vt2[0])
            v.tensor_mul(stat[:, 11:12], stat[:, 11:12], stat[:, 6:7]).then_inc(CV, 1)
            v.reduce_sum(out=part8[:, 0:8].rearrange("p (s o) -> p s o", o=1),
                         in_=g_trash[:].rearrange("p (s c) -> p s c", s=8),
                         axis=AX.X).then_inc(CV, 1)
            v.reduce_sum(out=part8[:, 8:16].rearrange("p (s o) -> p s o", o=1),
                         in_=g_trash2[:].rearrange("p (s c) -> p s c", s=8),
                         axis=AX.X).then_inc(CV, 1)
            v.wait_ge(CV, vt2[3])
            v.reduce_sum(out=stat[:, 7:8], in_=part8[:, 0:8], axis=AX.X).then_inc(CV, 1)
            v.reduce_sum(out=stat[:, 8:9], in_=part8[:, 8:16], axis=AX.X).then_inc(CV, 1)
            v.tensor_mul(stat[:, 9:10], stat[:, 2:3], tokT[:, 0:1]).then_inc(CV, 1)
            v.tensor_mul(stat[:, 10:11], stat[:, 2:3], tokT[:, 1:2]).then_inc(CV, 1)
            v.wait_ge(CV, vt2[7])
            v.tensor_sub(stat[:, 9:10], stat[:, 7:8], stat[:, 9:10]).then_inc(CV, 1)
            v.tensor_sub(stat[:, 10:11], stat[:, 8:9], stat[:, 10:11]).then_inc(CV, 1)
            v.wait_ge(CV, vt2[9])
            v.tensor_mul(s_sb[:, st:st + 1], stat[:, 9:10], stat[:, 6:7]).then_inc(CV, 1)
            v.tensor_mul(s_sb[:, 32 + st:32 + st + 1], stat[:, 10:11],
                         stat[:, 6:7]).then_inc(CV, 1)
            return None
        on("vector", murfin, waits=[("chain_s", s2[1]), ("chain_g", gt[1])])
        s3 = kb.tks("scalar", 2)
        def r2fn(s, s3=s3):
            s.activation(out=stat[:, 12:13], in_=stat[:, 11:12], func=AF.Ln,
                         bias=ccol[:, 0:1]).then_inc(CS, 1)
            s.wait_ge(CS, s3[0])
            return s.activation(out=stat[:, 12:13], in_=stat[:, 12:13], func=AF.Exp,
                                scale=-0.5).then_inc(CS, 1)
        on("scalar", r2fn, waits=[("chain_v", vt2[1])])
        vt3 = kb.tks("vector", 3)
        def scfn(v, vt3=vt3):
            v.tensor_mul(stat[:, 13:14], stat[:, 6:7], stat[:, 12:13]).then_inc(CV, 1)
            v.wait_ge(CV, vt3[0])
            v.tensor_mul(stat[:, 14:15], stat[:, 2:3], stat[:, 13:14]).then_inc(CV, 1)
            v.wait_ge(CV, vt3[1])
            return v.tensor_scalar_mul(stat[:, 14:15], stat[:, 14:15], -1.0).then_inc(CV, 1)
        on("vector", scfn, waits=[("chain_s", s3[1])])
        s4, = kb.tks("scalar", 1)
        uwl = [("chain_v", vt3[2])]
        if st >= 2:
            uwl.append(prev[st - 2]["u2ncons"])
        on("scalar", lambda s, buf=buf, st=st: s.activation(
            out=u2n_t[st % 2][:], in_=buf[:], func=AF.Identity,
            scale=stat[:, 13:14], bias=stat[:, 14:15]).then_inc(CS, 1), waits=uwl)
        twl = [("chain_s", s4)]
        if st >= 1:
            twl.append(("chain_v", prev[st - 1]["u2Tev"]))
        for c in range(8):
            on("tensor", lambda t, c=c, st=st: t.transpose(
                PBB[0][:, c * 128:(c + 1) * 128], u2n_t[st % 2][:, c * 128:(c + 1) * 128],
                identb[:]), waits=twl if c == 0 else [], inc=(1 if c == 7 else 0))
        t_u2tr = kb.tok("tensor")
        vt4 = kb.tks("vector", 8)
        def u2ev(v, st=st):
            last = None
            for c in range(8):
                last = v.tensor_copy(
                    out=u2T[(st // 4) % 2][:, c * 512 + (st % 4) * 128:
                                           c * 512 + (st % 4) * 128 + 128],
                    in_=PBB[0][:, c * 128:(c + 1) * 128]).then_inc(CV, 1)
            return last
        on("vector", u2ev, waits=[t_u2tr])
        prev[st] = dict(sq=s1, sqred=vt[1], bufred=vt[0], dots=gt[1],
                        dotred=vt2[3], s=vt2[11], u2n=s4,
                        u2ncons=("tensor", t_u2tr[1]), u2Tev=vt4[7])

    def emit_qkv(g):
        wl = [("chain_v", prev[4 * g + 3]["u2Tev"]), t_wqkv, t_wout,
              ("chain_v", kb.ccnt["vector"])]
        if g >= 2 and (4 * (g - 2) + 2) in t_att:
            wl.append(t_att[4 * (g - 2) + 2])
            wl.append(t_att[4 * (g - 2) + 3])
        first = True
        for mi in range(12):
            pb = PB[3 + (mi % 2)]
            for kc in range(8):
                on("tensor", lambda t, mi=mi, kc=kc, pb=pb, g=g: t.matmul(
                    pb[:, 0:512],
                    Wqkv_sb[:, kc * 1536 + mi * 128:kc * 1536 + mi * 128 + 128],
                    u2T[g % 2][:, kc * 512:(kc + 1) * 512],
                    start=(kc == 0), stop=(kc == 7)),
                   waits=(wl if first else ([] if kc else [("chain_v", kb.ccnt["vector"])])),
                   inc=(1 if kc == 7 else 0))
                first = False
            tmm = kb.tok("tensor")
            if mi < 4:
                vr = kb.tks("vector", 1)
                on("vector", lambda v, mi=mi, pb=pb, g=g: v.tensor_scalar(
                    out=qbuf[g % 2][:, mi * 512:(mi + 1) * 512], in0=pb[:, 0:512],
                    scalar1=bqkv[:, mi:mi + 1], scalar2=None,
                    op0=ALU.add).then_inc(CV, 1), waits=[tmm])
            elif mi < 8:
                vr = kb.tks("vector", 4)
                def kev(v, mi=mi, pb=pb, g=g):
                    last = None
                    for j in range(4):
                        last = v.tensor_scalar(
                            out=kring[(4 * g + j) % RING][:, (mi - 4) * 128:(mi - 3) * 128],
                            in0=pb[:, j * 128:(j + 1) * 128],
                            scalar1=bqkv[:, mi:mi + 1], scalar2=None,
                            op0=ALU.add).then_inc(CV, 1)
                    return last
                on("vector", kev, waits=[tmm])
            else:
                vr = kb.tks("vector", 1)
                on("vector", lambda v, pb=pb, mi=mi: v.tensor_scalar(
                    out=vtmpb[:], in0=pb[:, 0:512], scalar1=bqkv[:, mi:mi + 1],
                    scalar2=None, op0=ALU.add).then_inc(CV, 1), waits=[tmm])
                tvv = ("chain_v", vr[0])
                for j in range(4):
                    on("tensor", lambda t, j=j: t.transpose(
                        PBB[1][:, 0:128], vtmpb[:, j * 128:(j + 1) * 128], identb[:]),
                       waits=[tvv] if j == 0 else [("chain_v", kb.ccnt["vector"])], inc=1)
                    tmt = kb.tok("tensor")
                    vr2 = kb.tks("vector", 1)
                    on("vector", lambda v, j=j, mi=mi, g=g: v.tensor_copy(
                        out=vring[(4 * g + j) % RING][:, (mi - 8) * 128:(mi - 7) * 128],
                        in_=PBB[1][:, 0:128]).then_inc(CV, 1), waits=[tmt])
        t_qkv[g] = fence("vector")

    def emit_light(ti):
        kw = min(max(ti * 128 - 128, 0), N - 384)
        var = 0 if ti == 0 else (2 if ti == 31 else 1)
        g_of = lambda s_: s_ // 4
        need = [t_qkv[g_of(min(max(kw // 128 + j, 0), 31))] for j in range(3)]
        need.append(t_qkv[g_of(ti)])
        if (ti // 4 - 1, "out_pe") in t_att:
            need.append(t_att[(ti // 4 - 1, "out_pe")])
        if light_state["prev_exp2"] is not None:
            need.append(("chain_s", light_state["prev_exp2"]))
            need.append(("chain_v", kb.ccnt["vector"]))
        for h in range(H):
            prev_exp2 = light_state["prev_exp2"]
            for j in range(3):
                S = kw // 128 + j
                swl = []
                if h == 0 and j == 0:
                    swl = list(need)
                elif j == 0 and prev_exp2 is not None:
                    swl = [("chain_s", prev_exp2), ("chain_v", kb.ccnt["vector"])]
                on("tensor", lambda t, h=h, j=j, S=S, ti=ti: t.matmul(
                    PB[5][:, j * 128:(j + 1) * 128],
                    qbuf[(ti // 4) % 2][64 * (h % 2):64 * (h % 2) + 64,
                                        (h // 2) * 512 + (ti % 4) * 128:
                                        (h // 2) * 512 + (ti % 4) * 128 + 128],
                    kring[S % RING][64 * (h % 2):64 * (h % 2) + 64,
                                    (h // 2) * 128:(h // 2) * 128 + 128],
                    start=(j == 0), stop=False, skip_group_check=True),
                   waits=swl, inc=0)
            for j in range(3):
                on("tensor", lambda t, var=var, j=j: t.matmul(
                    PB[5][:, j * 128:(j + 1) * 128], identb[:],
                    lmasks[:, var * 384 + j * 128:var * 384 + (j + 1) * 128],
                    start=False, stop=True, skip_group_check=True),
                   waits=[], inc=(1 if j == 2 else 0))
            tS = kb.tok("tensor")
            if "S11" in kb.debug and ti == 1 and h == 0:
                vrdbg = kb.tks("vector", 1)
                dbgS = kb.sb("dbgS", [128, 384], F32)
                on("vector", lambda v: v.tensor_copy(out=dbgS[:], in_=PB[5][:, 0:384]).then_inc(CV, 1),
                   waits=[tS])
                kb.tap("S11", dbgS[:], (128, 384))
            vr = kb.tks("vector", 1)
            on("vector", lambda v: v.reduce_max(out=scol[:, 0:1], in_=PB[5][:, 0:384],
                                                axis=AX.X, negate=True).then_inc(CV, 1),
               waits=[tS])
            sr = kb.tks("scalar", 1)
            on("scalar", lambda s: s.activation(out=P_u[:], in_=PB[5][:, 0:384],
                                                func=AF.Exp,
                                                bias=scol[:, 0:1]).then_inc(CS, 1),
               waits=[("chain_v", vr[0])])
            vr2 = kb.tks("vector", 2)
            def sum_fn(v, vr2=vr2):
                v.reduce_sum(out=scol[:, 1:2], in_=P_u[:], axis=AX.X).then_inc(CV, 1)
                return None
            on("vector", sum_fn, waits=[("chain_s", sr[0])])
            sr2 = kb.tks("scalar", 1)
            on("scalar", lambda s: s.activation(out=scol[:, 2:3], in_=scol[:, 1:2],
                                                func=AF.Ln).then_inc(CS, 1),
               waits=[("chain_v", vr2[0])])
            def bias2_fn(v, vr2=vr2):
                return v.tensor_sub(scol[:, 3:4], scol[:, 0:1],
                                    scol[:, 2:3]).then_inc(CV, 1)
            on("vector", bias2_fn, waits=[("chain_s", sr2[0])])
            sr3 = kb.tks("scalar", 1)
            on("scalar", lambda s: s.activation(out=Pbb[:], in_=PB[5][:, 0:384],
                                                func=AF.Exp,
                                                bias=scol[:, 3:4]).then_inc(CS, 1),
               waits=[("chain_v", vr2[1])])
            light_state["prev_exp2"] = sr3[0]
            for j in range(3):
                on("tensor", lambda t, j=j: t.transpose(
                    PBB[1][:, j * 128:(j + 1) * 128], Pbb[:, j * 128:(j + 1) * 128],
                    identb[:]), waits=[("chain_s", sr3[0])] if j == 0 else [],
                   inc=(1 if j == 2 else 0))
            tPT = kb.tok("tensor")
            vr3 = kb.tks("vector", 1)
            on("vector", lambda v: v.tensor_copy(out=PT_sb[:],
                                                 in_=PBB[1][:, 0:384]).then_inc(CV, 1),
               waits=[tPT])
            for j in range(3):
                S = kw // 128 + j
                on("tensor", lambda t, h=h, j=j, S=S: t.matmul(
                    PB[5][0:64, 384:512],
                    vring[S % RING][:, h * 64:(h + 1) * 64],
                    PT_sb[:, j * 128:(j + 1) * 128],
                    start=(j == 0), stop=(j == 2), skip_group_check=True),
                   waits=[("chain_v", vr3[0])] if j == 0 else [], inc=(1 if j == 2 else 0))
            tPV = kb.tok("tensor")
            vr4 = kb.tks("vector", 1)
            on("vector", lambda v, h=h, ti=ti: v.tensor_copy(
                out=attno[64 * (h % 2):64 * (h % 2) + 64,
                          (h // 2) * 512 + (ti % 4) * 128:(h // 2) * 512 + (ti % 4) * 128 + 128],
                in_=PB[5][0:64, 384:512]).then_inc(CV, 1), waits=[tPV])
        t_att[ti] = fence("vector")

    def emit_lightout(g):
        wl = [t_att[4 * g + 3], t_wout]
        if g >= 2:
            wl.append(t_spill_prev[g - 2])
        first = True
        for mi in range(8):
            pb = PB[3 + (mi % 2)]
            for kc in range(4):
                on("tensor", lambda t, mi=mi, kc=kc, pb=pb, g=g: t.matmul(
                    pb[:, 0:512],
                    Wout_sb[:, kc * 1024 + mi * 128:kc * 1024 + mi * 128 + 128],
                    attno[:, kc * 512:(kc + 1) * 512],
                    start=(kc == 0), stop=(kc == 3)),
                   waits=(wl if first else ([("chain_v", kb.ccnt["vector"])] if kc == 0 else [])),
                   inc=(1 if kc == 3 else 0))
                first = False
            tmm = kb.tok("tensor")
            vr0 = kb.tks("vector", 1)
            on("vector", lambda v, pb=pb: v.tensor_copy(out=loev[:],
                                                        in_=pb[:, 0:512]).then_inc(CV, 1),
               waits=[tmm])
            tle = ("chain_v", vr0[0])
            for j in range(4):
                on("tensor", lambda t, j=j: t.transpose(
                    PB[0][:, 128:256], loev[:, j * 128:(j + 1) * 128], identf[:]),
                   waits=[tle] if j == 0 else [("chain_v", kb.ccnt["vector"])], inc=1)
                tmt = kb.tok("tensor")
                vr = kb.tks("vector", 1)
                on("vector", lambda v, j=j, mi=mi, g=g: v.scalar_tensor_tensor(
                    out=GST[g % 2][:, j * 1024 + mi * 128:j * 1024 + mi * 128 + 128],
                    in0=PB[0][:, 128:256], scalar=1.0, in1=null_bc[:, mi * 128:(mi + 1) * 128],
                    op0=ALU.mult, op1=ALU.add).then_inc(CV, 1), waits=[tmt])
        t_att[(g, "out_pe")] = ("tensor", kb.cnt["tensor"])
        t_att[(g, "out")] = fence("vector")
        last = None
        for j in range(4):
            last = dma("sync", "spill",
                       d_aout.rearrange("(p t) f -> p t f", t=32)[:, 4 * g + j, :],
                       GST[g % 2][:, j * 1024:(j + 1) * 1024],
                       waits=[t_att[(g, "out")]])
        t_spill_prev[g] = last

    for g in range(8):
        for j in range(4):
            emit_subtile(4 * g + j)
        emit_qkv(g)
        if g == 0 and "qb0" in kb.debug:
            for nm, ap_, shp in [("qb0", qbuf[0][:, 0:2048], (128, 2048)),
                                 ("u2T0", u2T[0][:, 0:4096], (128, 4096)),
                                 ("kr0", kring[0][:, 0:512], (128, 512)),
                                 ("kr1", kring[1][:, 0:512], (128, 512))]:
                t_ = kb.nc.dram_tensor("tap_" + nm, list(shp), BF16, kind="ExternalOutput").ap()
                kb.taps[nm] = t_
                dma("sync", "dbgtap", t_, ap_, waits=[t_qkv[0]])
        if g >= 1:
            for j in range(4):
                emit_light(4 * (g - 1) + j)
            emit_lightout(g - 1)
    for j in range(4):
        emit_light(28 + j)
    emit_lightout(7)

    t_sready = fence("vector")
    kb.tap("s", s_sb[:], (128, 64))
    kb.tap("stat", stat[:], (128, 16))

    vtm = kb.tks("vector", 2)
    def smax_fn(v, vtm=vtm):
        v.reduce_max(out=tot2[:, 0:1], in_=s_sb[:, :32], axis=AX.X).then_inc(CV, 1)
        return v.reduce_max(out=tot2[:, 1:2], in_=s_sb[:, 32:], axis=AX.X).then_inc(CV, 1)
    on("vector", smax_fn, waits=[t_sready])
    tv = fence("vector")
    tt = on("tensor", lambda t: t.transpose(ps_tr[:2, :128], tot2[:], identf[:]),
            waits=[tv, t_if], inc=1)
    vtm = kb.tks("vector", 2)
    def smax2_fn(v, vtm=vtm):
        v.tensor_copy(out=scr2[:], in_=ps_tr[:2, :128]).then_inc(CV, 1)
        v.wait_ge(CV, vtm[0])
        return v.reduce_max(out=scr2b[:2, 0:1], in_=scr2[:2, :], axis=AX.X).then_inc(CV, 1)
    on("vector", smax2_fn, waits=[tt])
    tv = fence("vector")
    tt = on("tensor", lambda t: t.transpose(ps_row[:1, :2], scr2b[:2, 0:1], identf[:2, :2]),
            waits=[tv], inc=1)
    vtm = kb.tks("vector", 1)
    on("vector", lambda v: v.tensor_copy(out=smrow[:], in_=ps_row[:1, :2]).then_inc(CV, 1),
       waits=[tt])
    tv = fence("vector")

    curs = [4.0]
    for _ in range(19):
        curs.append(max(curs[-1] * 0.7, 0.03))
    a1 = float(np.float32(4.0) * np.float32(np.float32(LOGK) - np.float32(math.log(N))))
    vtm = kb.tks("vector", 3)
    def init_fn(v, vtm=vtm):
        v.memset(acur[:, 0:2], a1).then_inc(CV, 1)
        v.tensor_scalar_min(abrow[:1, 6:8], smrow[:1, :], -a1).then_inc(CV, 1)
        v.wait_ge(CV, vtm[1])
        return v.tensor_scalar_mul(abrow[:1, 2:4], abrow[:1, 6:8],
                                   -1.0 / curs[1]).then_inc(CV, 1)
    on("vector", init_fn, waits=[tv])
    tv = fence("vector")
    tt = on("tensor", lambda t: t.matmul(ps_bc[:, 0:2], ones1x[:], abrow[:1, 2:4],
                                         start=True, stop=True), waits=[tv], inc=1)
    vtm = kb.tks("vector", 1)
    on("vector", lambda v: v.tensor_copy(out=acur[:, 2:4], in_=ps_bc[:, 0:2]).then_inc(CV, 1),
       waits=[tt])
    tv = fence("vector")

    for it in range(1, 20):
        cur = curs[it]
        nxt = curs[it + 1] if it < 19 else None
        vr = kb.tks("vector", 3)
        def relu_fn(v, vr=vr):
            v.tensor_scalar(out=relu_t[:, :32], in0=s_sb[:, :32], scalar1=acur[:, 0:1],
                            scalar2=0.0, op0=ALU.add, op1=ALU.max).then_inc(CV, 1)
            v.tensor_scalar(out=relu_t[:, 32:], in0=s_sb[:, 32:], scalar1=acur[:, 1:2],
                            scalar2=0.0, op0=ALU.add, op1=ALU.max).then_inc(CV, 1)
            v.wait_ge(CV, vr[1])
            return v.tensor_sub(sbp[:], s_sb[:], relu_t[:]).then_inc(CV, 1)
        on("vector", relu_fn, waits=[tv])
        tv2 = fence("vector")
        sr = kb.tks("scalar", 2)
        def exp_fn(s, cur=cur):
            s.activation(out=expb[:, :32], in_=sbp[:, :32], func=AF.Exp,
                         scale=1.0 / cur, bias=acur[:, 2:3],
                         accum_out=esum[:, 0:1]).then_inc(CS, 1)
            return s.activation(out=expb[:, 32:], in_=sbp[:, 32:], func=AF.Exp,
                                scale=1.0 / cur, bias=acur[:, 3:4],
                                accum_out=esum[:, 1:2]).then_inc(CS, 1)
        on("scalar", exp_fn, waits=[tv2])
        ts2 = fence("scalar")
        vr = kb.tks("vector", 4)
        def esum_fn(v, vr=vr):
            v.reduce_sum(out=part8[:, 0:8].rearrange("p (s o) -> p s o", o=1),
                         in_=expb[:, :32].rearrange("p (s c) -> p s c", s=8),
                         axis=AX.X).then_inc(CV, 1)
            v.reduce_sum(out=part8[:, 8:16].rearrange("p (s o) -> p s o", o=1),
                         in_=expb[:, 32:].rearrange("p (s c) -> p s c", s=8),
                         axis=AX.X).then_inc(CV, 1)
            v.wait_ge(CV, vr[1])
            v.reduce_sum(out=esum[:, 0:1], in_=part8[:, 0:8], axis=AX.X).then_inc(CV, 1)
            return v.reduce_sum(out=esum[:, 1:2], in_=part8[:, 8:16], axis=AX.X).then_inc(CV, 1)
        on("vector", esum_fn, waits=[ts2])
        tv3 = fence("vector")
        tt = on("tensor", lambda t: t.matmul(ps_row[:1, 2:4], onescol[:], esum[:],
                                             start=True, stop=True), waits=[tv3], inc=1)
        vr = kb.tks("vector", 1)
        on("vector", lambda v: v.tensor_copy(out=abrow[:1, 0:2],
                                             in_=ps_row[:1, 2:4]).then_inc(CV, 1),
           waits=[tt])
        tv3 = fence("vector")
        sr = kb.tks("scalar", 1)
        on("scalar", lambda s: s.activation(out=abrow[:1, 4:6], in_=abrow[:1, 0:2],
                                            func=AF.Ln).then_inc(CS, 1), waits=[tv3])
        ts3 = fence("scalar")

        vr = kb.tks("vector", 6 if nxt is not None else 4)
        def anew_fn(v, cur=cur, nxt=nxt, vr=vr):
            v.tensor_scalar_mul(abrow[:1, 6:8], abrow[:1, 6:8], 1.0 / cur).then_inc(CV, 1)
            v.wait_ge(CV, vr[0])
            v.tensor_add(abrow[:1, 4:6], abrow[:1, 4:6], abrow[:1, 6:8]).then_inc(CV, 1)
            v.wait_ge(CV, vr[1])
            v.tensor_scalar(out=abrow[:1, 0:2], in0=abrow[:1, 4:6], scalar1=LOGK,
                            scalar2=-cur, op0=ALU.subtract, op1=ALU.mult).then_inc(CV, 1)
            v.wait_ge(CV, vr[2])
            if nxt is not None:
                v.tensor_scalar_mul(abrow[:1, 6:8], abrow[:1, 0:2], -1.0).then_inc(CV, 1)
                v.wait_ge(CV, vr[3])
                v.tensor_tensor(out=abrow[:1, 6:8], in0=abrow[:1, 6:8],
                                in1=smrow[:1, :], op=ALU.min).then_inc(CV, 1)
                v.wait_ge(CV, vr[4])
                return v.tensor_scalar_mul(abrow[:1, 2:4], abrow[:1, 6:8],
                                           -1.0 / nxt).then_inc(CV, 1)
            return v.tensor_scalar_mul(abrow[:1, 2:4], abrow[:1, 0:2], 0.0).then_inc(CV, 1)
        on("vector", anew_fn, waits=[ts3])
        tv4 = fence("vector")
        tt = on("tensor", lambda t: t.matmul(ps_bc[:, 0:4], ones1x[:], abrow[:1, 0:4],
                                             start=True, stop=True), waits=[tv4], inc=1)
        vr = kb.tks("vector", 1)
        on("vector", lambda v: v.tensor_copy(out=acur[:, 0:4],
                                             in_=ps_bc[:, 0:4]).then_inc(CV, 1),
           waits=[tt])
        tv = fence("vector")

    vr = kb.tks("vector", 4)
    def mask_fn(v, vr=vr):
        v.tensor_scalar(out=mask_t[:, :32], in0=s_sb[:, :32], scalar1=acur[:, 0:1],
                        scalar2=0.0, op0=ALU.add, op1=ALU.is_ge).then_inc(CV, 1)
        v.tensor_scalar(out=mask_t[:, 32:], in0=s_sb[:, 32:], scalar1=acur[:, 1:2],
                        scalar2=0.0, op0=ALU.add, op1=ALU.is_ge).then_inc(CV, 1)
        v.wait_ge(CV, vr[1])
        v.tensor_copy(out=maskb[:, :32], in_=mask_t[:, :32]).then_inc(CV, 1)
        return v.tensor_copy(out=maskb[:, 32:], in_=mask_t[:, 32:]).then_inc(CV, 1)
    on("vector", mask_fn, waits=[tv])
    tv = fence("vector")
    on("tensor", lambda t: t.matmul(ps_rank[:], trib_sb[:], maskb[:],
                                    start=True, stop=True), waits=[tv, t_tri], inc=0)
    tt = on("tensor", lambda t: t.matmul(ps_tot[:1, :], onesbcol[:], maskb[:],
                                         start=True, stop=True), waits=[t_ob], inc=1)
    vr = kb.tks("vector", 9)
    def rank_fn(v, vr=vr):
        v.tensor_copy(out=mrank[:], in_=ps_rank[:]).then_inc(CV, 1)
        v.tensor_copy(out=srow[:1, :], in_=ps_tot[:1, :]).then_inc(CV, 1)
        v.wait_ge(CV, vr[1])
        v.tensor_scalar_sub(mrank_m1[:], mrank[:], 1.0).then_inc(CV, 1)
        for r in range(2):
            sl = slice(r * 32, r * 32 + 32)
            v.tensor_tensor_scan(out=basef[:1, sl], data0=srow[:1, sl],
                                 data1=srow[:1, sl], initial=0.0,
                                 op0=ALU.add, op1=ALU.bypass).then_inc(CV, 1)
        v.wait_ge(CV, vr[4])
        for r in range(2):
            v.tensor_copy(out=srow[:1, r * 32 + 1:r * 32 + 32],
                          in_=basef[:1, r * 32:r * 32 + 31]).then_inc(CV, 1)
        v.memset(srow[:1, 0:1], 0.0)
        v.memset(srow[:1, 32:33], 0.0)
        v.wait_ge(CV, vr[6])
        v.tensor_scalar_mul(basef[:1, :], srow[:1, :], 2.0).then_inc(CV, 1)
        v.wait_ge(CV, vr[7])
        return v.tensor_copy(out=base_i[:1, :], in_=basef[:1, :]).then_inc(CV, 1)
    on("vector", rank_fn, waits=[tt], inc=0)
    tv = fence("vector")
    kb.tap("mask_t", mask_t[:], (128, 64))
    kb.tap("mrank", mrank[:], (128, 64))
    kb.tap("basef", srow[:1, :], (1, 64))

    pair_done = {}
    for ri, r in enumerate(("q", "k")):
        t_ct_prev = None
        for t_i in range(32):
            cb = Ct[t_i % 2]
            col = ri * 32 + t_i
            vr = kb.tks("vector", 1)
            def ct_fn(v, cb=cb, col=col):
                return v.tensor_scalar(out=cb[:], in0=iota_sb[:],
                                       scalar1=mrank_m1[:, col:col + 1],
                                       scalar2=mask_t[:, col:col + 1],
                                       op0=ALU.is_equal, op1=ALU.mult).then_inc(CV, 1)
            wl = [tv, t_iota]
            if t_ct_prev is not None:
                wl.append(t_ct_prev)
            on("vector", ct_fn, waits=wl)
            tvc = ("chain_v", vr[0])
            mwl = [tvc, t_cpt]
            if ri == 1 and t_i == 0:
                mwl.append(pair_done["q"])
            tmm = on("tensor", lambda t, cb=cb, t_i=t_i: t.matmul(
                ps_pair[:, 2 * t_i:2 * t_i + 2], cb[:],
                cpt_sb[:, 2 * t_i:2 * t_i + 2],
                start=True, stop=True), waits=mwl, inc=1)
            t_ct_prev = tmm
        vr = kb.tks("vector", 1)
        on("vector", lambda v, r=r: v.tensor_copy(out=pairs_f[r][:],
                                                  in_=ps_pair[:]).then_inc(CV, 1),
           waits=[t_ct_prev], inc=0)
        pair_done[r] = fence("vector")

    dynregs = {"sync": nc.alloc_register(mybir.EngineType.SP, "dyn_sy"),
               "scalar": nc.alloc_register(mybir.EngineType.Activation, "dyn_sc"),
               "gpsimd": nc.alloc_register(mybir.EngineType.Pool, "dyn_gp")}
    qcycle = ("sync", "scalar", "gpsimd")
    prev_write = {"q": None, "k": None}
    wcount = 0
    for t_i in range(32):
        for ri, r in enumerate(("q", "k")):
            col = ri * 32 + t_i
            e = qcycle[wcount % 3]
            wcount += 1
            reg = dynregs[e]
            def wfn(eng, r=r, col=col, t_i=t_i, reg=reg):
                eng.reg_load(reg, base_i[0:1, col:col + 1])
                ap = bass.AP(d_pairs[r].tensor, reg, [[2, 128], [1, 2]])
                return eng.dma_start(out=ap, in_=pairs_f[r][:, 2 * t_i:2 * t_i + 2])
            ch = "pw_" + r + "_" + e
            kb.chan[ch] = kb.chan.get(ch, 0) + 16
            wl = [pair_done["q"], pair_done["k"], tv]
            if prev_write[r] is not None:
                wl.append(prev_write[r])
            kb.q[e].append((wfn, tuple(wl), ("dma", ch)))
            prev_write[r] = ("ch", ch, kb.chan[ch])
    t_pairs_q = prev_write["q"]
    t_pairs_k = prev_write["k"]

    t_idx = {}
    for ri, r in enumerate(("q", "k")):
        t_idx[r] = dma("sync", "idx", idxf[:, ri * 16:(ri + 1) * 16],
                       d_pairs[r][0:1024, :].rearrange("(k p) v -> p k v", p=128),
                       waits=[t_pairs_q, t_pairs_k])
    vr = kb.tks("vector", 2)
    def idxc_fn(v):
        v.tensor_copy(out=idxs["q"][:], in_=idxf[:, :16]).then_inc(CV, 1)
        return v.tensor_copy(out=idxs["k"][:], in_=idxf[:, 16:]).then_inc(CV, 1)
    on("vector", idxc_fn, waits=[kb.chfinal("idx")])
    t_idxc = fence("vector")
    kb.tap("idx_q", idxs["q"][:], (128, 16), I32)
    kb.tap("idx_k", idxs["k"][:], (128, 16), I32)

    xqT = Abf[:, 0:8192]
    ctxT = Abf[:, 8192:16384]
    qTb = Abf[:, 16384:20480]
    kTb = Abf[:, 20480:24580]
    v_sm = Abf[:, 24580:28676]
    v_nul = Abf[:, 28676:29188]
    gstage = [CUR[:, 16384 + j * 1024:16384 + (j + 1) * 1024] for j in range(2)]
    ropeT = {"q": CUR[:, 18432:19456], "k": CUR[:, 19456:20480]}
    rt_rope = {"q": CUR[:, 20480:21504], "k": CUR[:, 21504:22528]}
    P_u_h = CUR[:, 22528:23553]
    scol_h = CUR[:, 23560:23568]
    rscr = CUR[:, 23600:24624]
    nullT = CUR[0:64, 24624:24640]
    nvrow = CUR[0:1, 24640:25152]
    hvb = [CUR[:, 25152:26176], CUR[:, 26176:27200]]
    Pbb_h = ob(21632, 22784)
    PT_h = ob(22784, 23936)
    aofm = ob(23936, 28032)
    how_b = ob(28032, 32128)
    hqw = WREGb[:, 0:4096]
    kvwT = WREGb[:, 4096:12288]
    hg_bc = tok_bc[:, 0:1024]

    t_light_done = t_att[(7, "out")]
    t_light_pe = t_att[(7, "out_pe")]
    t_spill = kb.chfinal("spill")

    dma("sync", "hgb", hg_bc, w["heavy_gamma"].unsqueeze(0).partition_broadcast(128),
        waits=[("chain_g", prev[31]["dots"])])
    t_hgb = kb.chfinal("hgb")

    t_wc = t_qkv[7]
    for kc in range(8):
        tl = dma("sync", "hwl", wtmp[:, 0:512], w["heavy_q_w"][kc * 128:(kc + 1) * 128, :],
                 waits=[t_wc])
        vr = kb.tks("vector", 1)
        on("vector", lambda v, kc=kc: v.tensor_copy(
            out=hqw[:, kc * 512:(kc + 1) * 512], in_=wtmp[:, 0:512]).then_inc(CV, 1),
            waits=[tl])
        t_wc = fence("vector")
    for h in range(H):
        tl = dma("sync", "hwl", wtmp[:, 0:1024], w["heavy_kv_w"][h], waits=[t_wc])
        vr = kb.tks("vector", 1)
        on("vector", lambda v: v.tensor_copy(out=u2n_t[0][:],
                                             in_=wtmp[:, 0:1024]).then_inc(CV, 1),
           waits=[tl])
        t_cast = fence("vector")
        for c in range(8):
            on("tensor", lambda t, c=c: t.transpose(
                PBB[1][:, c * 128:(c + 1) * 128], u2n_t[0][:, c * 128:(c + 1) * 128],
                identb[:]), waits=[t_cast] if c == 0 else [], inc=(1 if c == 7 else 0))
        tt = kb.tok("tensor")
        vr = kb.tks("vector", 1)
        on("vector", lambda v, h=h: v.tensor_copy(
            out=kvwT[:, h * 1024:(h + 1) * 1024], in_=PBB[1][:, 0:1024]).then_inc(CV, 1),
            waits=[tt])
        t_wc = fence("vector")
    for kc in range(4):
        tl = dma("sync", "hwl", wtmp[:, 0:1024],
                 w["heavy_out_w"][kc * 128:(kc + 1) * 128, :],
                 waits=[t_wc, t_light_pe, t_light_done])
        vr = kb.tks("vector", 1)
        on("vector", lambda v, kc=kc: v.tensor_copy(
            out=how_b[:, kc * 1024:(kc + 1) * 1024], in_=wtmp[:, 0:1024]).then_inc(CV, 1),
            waits=[tl])
        t_wc = fence("vector")
    t_hw = t_wc

    dma("sync", "hwl2", nullT, w["heavy_null_kv"].rearrange("t h d -> d (t h)"),
        waits=[t_light_done], allow_slow_non_contiguous=True)
    dma("sync", "hwl2", nvrow, w["heavy_null_kv"].rearrange("t h d -> t (h d)")[1:2, :],
        waits=[t_light_done])
    t_nl = kb.chfinal("hwl2")
    vr = kb.tks("vector", 10)
    def nullw_fn(v, vr=vr):
        last = None
        for h in range(H):
            last = v.tensor_copy(
                out=kTb[64 * (h % 2):64 * (h % 2) + 64,
                        (h // 2) * 1025 + 1024:(h // 2) * 1025 + 1025],
                in_=nullT[:, h:h + 1]).then_inc(CV, 1)
        v.memset(v_nul[:], 0.0).then_inc(CV, 1)
        v.wait_ge(CV, vr[8])
        return v.tensor_copy(out=v_nul[0:1, :], in_=nvrow).then_inc(CV, 1)
    on("vector", nullw_fn, waits=[t_nl, t_spill])
    t_null = fence("vector")

    for ri, r in enumerate(("q", "k")):
        for g in range(8):
            col = idxs[r][:, 2 * g:2 * g + 1]
            idma("gath", rt_rope[r][:, g * 128:(g + 1) * 128], None, cst["hrope"],
                 IndirectOffsetOnAxis(col, 0), waits=[t_idxc, t_light_done])
    t_gath = kb.chfinal("gath")
    t_rp = None
    for ri, r in enumerate(("q", "k")):
        for g in range(8):
            wl = [t_gath, t_light_done] if t_rp is None else [t_rp]
            on("tensor", lambda t, r=r, g=g: t.transpose(
                PB[5][:, 0:128], rt_rope[r][:, g * 128:(g + 1) * 128], identf[:]),
               waits=wl, inc=1)
            tt = kb.tok("tensor")
            vr = kb.tks("vector", 1)
            if r == "q":
                on("vector", lambda v, g=g: v.tensor_scalar_mul(
                    ropeT["q"][:, g * 128:(g + 1) * 128], PB[5][:, 0:128],
                    0.125).then_inc(CV, 1), waits=[tt])
            else:
                on("vector", lambda v, g=g: v.tensor_copy(
                    out=ropeT["k"][:, g * 128:(g + 1) * 128],
                    in_=PB[5][:, 0:128]).then_inc(CV, 1), waits=[tt])
            t_rp = ("chain_v", vr[0])
    t_ropeT = fence("vector")

    hstat = CUR[:, 23568:23600]
    wv = {}
    for i in range(16):
        ri, g = i // 8, i % 8
        r = "qk"[0] if ri == 0 else "k"
        col = idxs["q" if ri == 0 else "k"][:, 2 * g:2 * g + 1]
        wl = [t_idxc, t_light_done]
        if i >= 2:
            wl.append(("chain_v", wv[i - 2]["xn"]))
        tg = idma(f"gx{i % 2}", gstage[i % 2], None, x_h,
                  IndirectOffsetOnAxis(col, 0), waits=wl)
        swl = [tg]
        if i == 0:
            swl.append(t_light_done)
        if i >= 1:
            swl.append(("chain_v", wv[i - 1]["red"]))
        s1, = kb.tks("scalar", 1)
        on("scalar", lambda s, i=i: s.activation(
            out=act_trash[:], in_=gstage[i % 2][:], func=AF.Square).then_inc(CS, 1),
            waits=swl)
        w4 = 4 * (i % 2)
        vt = kb.tks("vector", 9)
        def wstats(v, i=i, w4=w4, vt=vt):
            v.reduce_sum(out=part8[:, 0:8].rearrange("p (s o) -> p s o", o=1),
                         in_=gstage[i % 2][:].rearrange("p (s c) -> p s c", s=8),
                         axis=AX.X).then_inc(CV, 1)
            v.reduce_sum(out=part8[:, 8:16].rearrange("p (s o) -> p s o", o=1),
                         in_=act_trash[:].rearrange("p (s c) -> p s c", s=8),
                         axis=AX.X).then_inc(CV, 1)
            v.wait_ge(CV, vt[1])
            v.reduce_sum(out=hstat[:, w4:w4 + 1], in_=part8[:, 0:8], axis=AX.X).then_inc(CV, 1)
            v.reduce_sum(out=hstat[:, w4 + 1:w4 + 2], in_=part8[:, 8:16], axis=AX.X).then_inc(CV, 1)
            v.wait_ge(CV, vt[3])
            v.tensor_scalar_mul(hstat[:, w4:w4 + 1], hstat[:, w4:w4 + 1],
                                1.0 / D).then_inc(CV, 1)
            v.tensor_scalar_mul(hstat[:, w4 + 1:w4 + 2], hstat[:, w4 + 1:w4 + 2],
                                1.0 / D).then_inc(CV, 1)
            v.wait_ge(CV, vt[5])
            v.tensor_mul(hstat[:, w4 + 2:w4 + 3], hstat[:, w4:w4 + 1],
                         hstat[:, w4:w4 + 1]).then_inc(CV, 1)
            v.wait_ge(CV, vt[6])
            v.tensor_sub(hstat[:, w4 + 2:w4 + 3], hstat[:, w4 + 1:w4 + 2],
                         hstat[:, w4 + 2:w4 + 3]).then_inc(CV, 1)
            v.wait_ge(CV, vt[7])
            return v.tensor_scalar(out=hstat[:, w4 + 2:w4 + 3],
                                   in0=hstat[:, w4 + 2:w4 + 3], scalar1=1e-30,
                                   scalar2=None, op0=ALU.max).then_inc(CV, 1)
        on("vector", wstats, waits=[("chain_s", s1)])
        s2 = kb.tks("scalar", 2)
        def wr_fn(s, w4=w4, s2=s2):
            s.activation(out=hstat[:, w4 + 3:w4 + 4], in_=hstat[:, w4 + 2:w4 + 3],
                         func=AF.Ln).then_inc(CS, 1)
            s.wait_ge(CS, s2[0])
            return s.activation(out=hstat[:, w4 + 3:w4 + 4], in_=hstat[:, w4 + 3:w4 + 4],
                                func=AF.Exp, scale=-0.5).then_inc(CS, 1)
        on("scalar", wr_fn, waits=[("chain_v", vt[8])])
        vx = kb.tks("vector", 2)
        xwl = [("chain_s", s2[1]), t_hgb]
        if i >= 1:
            xwl.append(("chain_v", wv[i - 1]["xn"]))
            xwl.append(wv[i - 1]["tr"])
        def wxn(v, i=i, w4=w4, vx=vx):
            v.tensor_scalar(out=g_trash[:], in0=gstage[i % 2][:],
                            scalar1=hstat[:, w4:w4 + 1],
                            scalar2=hstat[:, w4 + 3:w4 + 4],
                            op0=ALU.subtract, op1=ALU.mult).then_inc(CV, 1)
            v.wait_ge(CV, vx[0])
            return v.tensor_mul(u2n_t[1][:], g_trash[:], hg_bc).then_inc(CV, 1)
        on("vector", wxn, waits=xwl)
        twl = [("chain_v", vx[1])]
        if i >= 1:
            twl.append(("chain_v", wv[i - 1]["ev"]))
        for c in range(8):
            on("tensor", lambda t, c=c: t.transpose(
                PBB[0][:, c * 128:(c + 1) * 128], u2n_t[1][:, c * 128:(c + 1) * 128],
                identb[:]), waits=twl if c == 0 else [], inc=(1 if c == 7 else 0))
        t_tr = kb.tok("tensor")
        vev = kb.tks("vector", 8)
        def wev(v, ri=ri, g=g):
            last = None
            base = 0 if ri == 0 else 8192
            for c in range(8):
                last = v.tensor_copy(
                    out=Abf[:, base + c * 1024 + g * 128:base + c * 1024 + g * 128 + 128],
                    in_=PBB[0][:, c * 128:(c + 1) * 128]).then_inc(CV, 1)
            return last
        on("vector", wev, waits=[t_tr, t_spill])
        wv[i] = dict(red=vt[1], xn=vx[1], tr=t_tr, ev=vev[7])
        if i == 7:
            t_xqT = fence("vector")
    t_ctxT = fence("vector")

    pe_evac = {0: None, 1: None}
    def rope_evac(pb, hs, csrc, dst, dcol, vr6):
        def fn(v, pb=pb, hs=hs, csrc=csrc, dst=dst, dcol=dcol, vr6=vr6):
            A = pb[hs:hs + 32, 0:512]
            B = pb[hs + 32:hs + 64, 0:512]
            c1 = csrc[0:32, :]
            c2 = csrc[32:64, :]
            sn1 = csrc[64:96, :]
            sn2 = csrc[96:128, :]
            v.tensor_mul(rscr[0:32, 0:512], A, c1).then_inc(CV, 1)
            v.tensor_mul(rscr[0:32, 512:1024], B, sn1).then_inc(CV, 1)
            v.wait_ge(CV, vr6[1])
            v.tensor_sub(dst[hs:hs + 32, dcol:dcol + 512], rscr[0:32, 0:512],
                         rscr[0:32, 512:1024]).then_inc(CV, 1)
            v.tensor_mul(rscr[32:64, 0:512], B, c2).then_inc(CV, 1)
            v.tensor_mul(rscr[32:64, 512:1024], A, sn2).then_inc(CV, 1)
            v.wait_ge(CV, vr6[4])
            return v.tensor_add(dst[hs + 32:hs + 64, dcol:dcol + 512],
                                rscr[32:64, 0:512],
                                rscr[32:64, 512:1024]).then_inc(CV, 1)
        return fn

    for oc in range(4):
        for half in range(2):
            pbi = (2 * oc + half) % 2
            pb = PB[pbi]
            wl0 = [t_xqT, t_hw, t_ropeT]
            if pe_evac[pbi] is not None:
                wl0.append(("chain_v", pe_evac[pbi]))
            for kc in range(8):
                on("tensor", lambda t, oc=oc, half=half, kc=kc, pb=pb: t.matmul(
                    pb[:, 0:512], hqw[:, kc * 512 + oc * 128:kc * 512 + oc * 128 + 128],
                    xqT[:, kc * 1024 + half * 512:kc * 1024 + half * 512 + 512],
                    start=(kc == 0), stop=(kc == 7)),
                   waits=(wl0 if kc == 0 else []), inc=(1 if kc == 7 else 0))
            tmm = kb.tok("tensor")
            csrc = ropeT["q"][:, half * 512:half * 512 + 512]
            for hs in (0, 64):
                vr6 = kb.tks("vector", 6)
                on("vector", rope_evac(pb, hs, csrc, qTb, oc * 1024 + half * 512, vr6),
                   waits=[tmm] if hs == 0 else [])
            pe_evac[pbi] = kb.ccnt["vector"]
    for h in range(H):
        for half in range(2):
            pbi = half
            pb = PB[pbi]
            wl0 = [t_ctxT]
            if pe_evac[pbi] is not None:
                wl0.append(("chain_v", pe_evac[pbi]))
            for kc in range(8):
                on("tensor", lambda t, h=h, half=half, kc=kc, pb=pb: t.matmul(
                    pb[:, 0:512], kvwT[:, h * 1024 + kc * 128:h * 1024 + kc * 128 + 128],
                    ctxT[:, kc * 1024 + half * 512:kc * 1024 + half * 512 + 512],
                    start=(kc == 0), stop=(kc == 7)),
                   waits=(wl0 if kc == 0 else []), inc=(1 if kc == 7 else 0))
            tmm = kb.tok("tensor")
            csrc = ropeT["k"][:, half * 512:half * 512 + 512]
            vr6 = kb.tks("vector", 6)
            def kev(v, pb=pb, csrc=csrc, h=h, half=half, vr6=vr6):
                hb = 64 * (h % 2)
                kc0 = (h // 2) * 1025 + half * 512
                A = pb[0:32, 0:512]
                B = pb[32:64, 0:512]
                v.tensor_mul(rscr[0:32, 0:512], A, csrc[0:32, :]).then_inc(CV, 1)
                v.tensor_mul(rscr[0:32, 512:1024], B, csrc[64:96, :]).then_inc(CV, 1)
                v.wait_ge(CV, vr6[1])
                v.tensor_sub(kTb[hb:hb + 32, kc0:kc0 + 512],
                             rscr[0:32, 0:512], rscr[0:32, 512:1024]).then_inc(CV, 1)
                v.tensor_mul(rscr[32:64, 0:512], B, csrc[32:64, :]).then_inc(CV, 1)
                v.tensor_mul(rscr[32:64, 512:1024], A, csrc[96:128, :]).then_inc(CV, 1)
                v.wait_ge(CV, vr6[4])
                return v.tensor_add(kTb[hb + 32:hb + 64, kc0:kc0 + 512],
                                    rscr[32:64, 0:512],
                                    rscr[32:64, 512:1024]).then_inc(CV, 1)
            on("vector", kev, waits=[tmm])
            vr1 = kb.tks("vector", 1)
            on("vector", lambda v, pb=pb: v.tensor_copy(
                out=u2n_t[1][0:64, 0:512], in_=pb[64:128, 0:512]).then_inc(CV, 1),
               waits=[])
            pe_evac[pbi] = kb.ccnt["vector"]
            for j in range(4):
                on("tensor", lambda t, j=j: t.transpose(
                    PBB[0][:, j * 64:(j + 1) * 64],
                    u2n_t[1][0:64, j * 128:(j + 1) * 128], identb[0:64, 0:64]),
                   waits=[("chain_v", vr1[0])] if j == 0 else [],
                   inc=(1 if j == 3 else 0))
            tvt = kb.tok("tensor")
            vr4 = kb.tks("vector", 4)
            def vev_fn(v, h=h, half=half):
                last = None
                for j in range(4):
                    ch = half * 4 + j
                    last = v.tensor_copy(
                        out=v_sm[:, ch * 512 + h * 64:ch * 512 + h * 64 + 64],
                        in_=PBB[0][:, j * 64:(j + 1) * 64]).then_inc(CV, 1)
                return last
            on("vector", vev_fn, waits=[tvt])
    t_proj = fence("vector")

    vrz = kb.tks("vector", 1)
    on("vector", lambda v: v.memset(Pbb_h[:], 0.0).then_inc(CV, 1),
       waits=[t_light_done])
    t_pbz = fence("vector")
    prev_exp2_h = None
    prev_aofm = None
    for qc in range(8):
        for h in range(H):
            wl0 = [t_proj, t_pbz, t_null]
            if prev_exp2_h is not None:
                wl0 = [("chain_s", prev_exp2_h), ("chain_v", prev_aofm)]
            for pi, (c0, c1) in enumerate(((0, 512), (512, 1024), (1024, 1025))):
                on("tensor", lambda t, qc=qc, h=h, c0=c0, c1=c1: t.matmul(
                    S_H[:, c0:c1],
                    qTb[(h % 2) * 64:(h % 2) * 64 + 64,
                        (h // 2) * 1024 + qc * 128:(h // 2) * 1024 + qc * 128 + 128],
                    kTb[64 * (h % 2):64 * (h % 2) + 64,
                        (h // 2) * 1025 + c0:(h // 2) * 1025 + c1],
                    start=True, stop=(pi == 2), skip_group_check=True),
                   waits=(wl0 if pi == 0 else []), inc=(1 if pi == 2 else 0))
            tS = kb.tok("tensor")
            vr = kb.tks("vector", 1)
            on("vector", lambda v: v.reduce_max(out=scol_h[:, 0:1], in_=S_H,
                                                axis=AX.X, negate=True).then_inc(CV, 1),
               waits=[tS])
            sr = kb.tks("scalar", 1)
            on("scalar", lambda s: s.activation(out=P_u_h, in_=S_H, func=AF.Exp,
                                                bias=scol_h[:, 0:1]).then_inc(CS, 1),
               waits=[("chain_v", vr[0])])
            vr2 = kb.tks("vector", 1)
            on("vector", lambda v: v.reduce_sum(out=scol_h[:, 1:2], in_=P_u_h,
                                                axis=AX.X).then_inc(CV, 1),
               waits=[("chain_s", sr[0])])
            sr2 = kb.tks("scalar", 1)
            on("scalar", lambda s: s.activation(out=scol_h[:, 2:3], in_=scol_h[:, 1:2],
                                                func=AF.Ln).then_inc(CS, 1),
               waits=[("chain_v", vr2[0])])
            vr3 = kb.tks("vector", 1)
            on("vector", lambda v: v.tensor_sub(scol_h[:, 3:4], scol_h[:, 0:1],
                                                scol_h[:, 2:3]).then_inc(CV, 1),
               waits=[("chain_s", sr2[0])])
            sr3 = kb.tks("scalar", 1)
            on("scalar", lambda s: s.activation(out=Pbb_h[:, 0:1025], in_=S_H,
                                                func=AF.Exp,
                                                bias=scol_h[:, 3:4]).then_inc(CS, 1),
               waits=[("chain_v", vr3[0])])
            prev_exp2_h = sr3[0]
            for b in range(9):
                dst = PBB[1][:, b * 128:(b + 1) * 128] if b < 8 else PBB[0][:, 0:128]
                on("tensor", lambda t, b=b, dst=dst: t.transpose(
                    dst, Pbb_h[:, b * 128:(b + 1) * 128], identb[:]),
                   waits=[("chain_s", sr3[0])] if b == 0 else [],
                   inc=(1 if b == 8 else 0))
            tPT = kb.tok("tensor")
            vr4 = kb.tks("vector", 2)
            def ptev(v):
                v.tensor_copy(out=PT_h[:, 0:1024], in_=PBB[1][:, 0:1024]).then_inc(CV, 1)
                return v.tensor_copy(out=PT_h[:, 1024:1152],
                                     in_=PBB[0][:, 0:128]).then_inc(CV, 1)
            on("vector", ptev, waits=[tPT])
            for ch in range(9):
                vsl = (v_sm[:, ch * 512 + h * 64:ch * 512 + h * 64 + 64] if ch < 8
                       else v_nul[:, h * 64:h * 64 + 64])
                on("tensor", lambda t, ch=ch, vsl=vsl: t.matmul(
                    PV_H, vsl, PT_h[:, ch * 128:(ch + 1) * 128],
                    start=(ch == 0), stop=(ch == 8), skip_group_check=True),
                   waits=[("chain_v", vr4[1])] if ch == 0 else [],
                   inc=(1 if ch == 8 else 0))
            tPV = kb.tok("tensor")
            vr5 = kb.tks("vector", 1)
            on("vector", lambda v, h=h, qc=qc: v.tensor_copy(
                out=aofm[(h % 2) * 64:(h % 2) * 64 + 64,
                         (h // 2) * 1024 + qc * 128:(h // 2) * 1024 + qc * 128 + 128],
                in_=PV_H).then_inc(CV, 1), waits=[tPV])
            prev_aofm = vr5[0]
    t_attn_h = fence("vector")

    prev_scat = {}
    for sc in range(8):
        for half in range(2):
            pb = PB[half]
            wl0 = [t_attn_h]
            if pe_evac[half] is not None:
                wl0.append(("chain_v", pe_evac[half]))
            for a in range(4):
                on("tensor", lambda t, sc=sc, half=half, a=a, pb=pb: t.matmul(
                    pb[:, 0:512], aofm[:, a * 1024 + sc * 128:a * 1024 + sc * 128 + 128],
                    how_b[:, a * 1024 + half * 512:a * 1024 + half * 512 + 512],
                    start=(a == 0), stop=(a == 3)),
                   waits=(wl0 if a == 0 else []), inc=(1 if a == 3 else 0))
            tmm = kb.tok("tensor")
            vr = kb.tks("vector", 1)
            hwl = [tmm]
            if sc >= 2 and (sc - 2) in prev_scat:
                hwl.append(prev_scat[sc - 2])
            on("vector", lambda v, sc=sc, half=half, pb=pb: v.tensor_sub(
                hvb[sc % 2][:, half * 512:half * 512 + 512], pb[:, 0:512],
                null_bc[:, half * 512:half * 512 + 512]).then_inc(CV, 1),
               waits=hwl)
            pe_evac[half] = kb.ccnt["vector"]
        tev = ("chain_v", kb.ccnt["vector"])
        prev_scat[sc] = idma("scat", d_aout, IndirectOffsetOnAxis(
            idxs["q"][:, 2 * sc + 1:2 * sc + 2], 0), hvb[sc % 2][:, 0:1024], None,
            waits=[tev, t_spill], compute_op=ALU.add)
    t_scat = kb.chfinal("scat")

    kb.tap("xqT", xqT, (128, 8192), BF16)
    kb.tap("ctxT", ctxT, (128, 8192), BF16)
    kb.tap("qTb", qTb, (128, 4096), BF16)
    kb.tap("kTb", kTb, (128, 4100), BF16)
    kb.tap("vsm", v_sm, (128, 4096), BF16)
    kb.tap("aofm", aofm, (128, 4096), BF16)
    kb.tap("ropeTq", ropeT["q"], (128, 1024))
    kb.tap("hv0", hvb[0], (128, 1024))
    kb.tap("hv1", hvb[1], (128, 1024))

    wtmpb = wtmp[:].bitcast(BF16)
    tr_stat = stat
    tprev = {}
    for t in range(32):
        wl = [t_scat]
        if t >= 2:
            wl.append(("chain_v", tprev[t - 2]["vdone"]))
        ta = dma("sync", f"abld{t % 2}", wtmp[:, 0:1024],
                 d_aout.rearrange("(p t) f -> p t f", t=32)[:, t, :], waits=wl)
        tx = dma("sync", f"xbld{t % 2}", xs[t % 2][:], x_h[t * 128:(t + 1) * 128, :],
                 waits=wl)
        swl = [ta]
        if t >= 1:
            swl.append(("chain_v", tprev[t - 1]["red1"]))
        s1, = kb.tks("scalar", 1)
        on("scalar", lambda s: s.activation(out=act_trash[:], in_=wtmp[:, 0:1024],
                                            func=AF.Square).then_inc(CS, 1), waits=swl)

        def red_tree(v, src_bf, c_mu, vt):
            v.reduce_sum(out=part8[:, 0:8].rearrange("p (s o) -> p s o", o=1),
                         in_=src_bf.rearrange("p (s c) -> p s c", s=8),
                         axis=AX.X).then_inc(CV, 1)
            v.reduce_sum(out=part8[:, 8:16].rearrange("p (s o) -> p s o", o=1),
                         in_=act_trash[:].rearrange("p (s c) -> p s c", s=8),
                         axis=AX.X).then_inc(CV, 1)
            v.wait_ge(CV, vt[1])
            v.reduce_sum(out=tr_stat[:, c_mu:c_mu + 1], in_=part8[:, 0:8],
                         axis=AX.X).then_inc(CV, 1)
            v.reduce_sum(out=tr_stat[:, c_mu + 1:c_mu + 2], in_=part8[:, 8:16],
                         axis=AX.X).then_inc(CV, 1)
            v.wait_ge(CV, vt[3])
            v.tensor_scalar_mul(tr_stat[:, c_mu:c_mu + 1], tr_stat[:, c_mu:c_mu + 1],
                                1.0 / D).then_inc(CV, 1)
            v.tensor_scalar_mul(tr_stat[:, c_mu + 1:c_mu + 2],
                                tr_stat[:, c_mu + 1:c_mu + 2], 1.0 / D).then_inc(CV, 1)
            v.wait_ge(CV, vt[5])
            v.tensor_mul(tr_stat[:, c_mu + 2:c_mu + 3], tr_stat[:, c_mu:c_mu + 1],
                         tr_stat[:, c_mu:c_mu + 1]).then_inc(CV, 1)
            v.wait_ge(CV, vt[6])
            return v.tensor_sub(tr_stat[:, c_mu + 2:c_mu + 3],
                                tr_stat[:, c_mu + 1:c_mu + 2],
                                tr_stat[:, c_mu + 2:c_mu + 3]).then_inc(CV, 1)

        vt = kb.tks("vector", 8)
        on("vector", lambda v, vt=vt: red_tree(v, wtmp[:, 0:1024], 0, vt),
           waits=[("chain_s", s1)])
        s2 = kb.tks("scalar", 2)
        def rfn(s, s2=s2):
            s.activation(out=tr_stat[:, 3:4], in_=tr_stat[:, 2:3], func=AF.Ln,
                         bias=ccol[:, 0:1]).then_inc(CS, 1)
            s.wait_ge(CS, s2[0])
            return s.activation(out=tr_stat[:, 3:4], in_=tr_stat[:, 3:4],
                                func=AF.Exp, scale=-0.5).then_inc(CS, 1)
        on("scalar", rfn, waits=[("chain_v", vt[7])])
        vx = kb.tks("vector", 2)
        xwl = [("chain_s", s2[1]), tx]
        if t >= 1:
            xwl.append(("chain_v", tprev[t - 1]["xp"]))
            xwl.append(tprev[t - 1]["spill"])
        def xpfn(v, t=t, vx=vx):
            v.tensor_scalar(out=g_trash[:], in0=wtmp[:, 0:1024],
                            scalar1=tr_stat[:, 0:1], scalar2=tr_stat[:, 3:4],
                            op0=ALU.subtract, op1=ALU.mult).then_inc(CV, 1)
            v.wait_ge(CV, vx[0])
            return v.tensor_add(g_trash2[:], g_trash[:], xs[t % 2][:]).then_inc(CV, 1)
        on("vector", xpfn, waits=xwl)
        txs = dma("sync", f"xsp{t % 2}", d_xspill[t * 128:(t + 1) * 128, :],
                  g_trash2[:], waits=[("chain_v", vx[1])])
        s3, = kb.tks("scalar", 1)
        on("scalar", lambda s: s.activation(out=act_trash[:], in_=g_trash2[:],
                                            func=AF.Square).then_inc(CS, 1),
           waits=[("chain_v", vx[1])])
        vt2 = kb.tks("vector", 8)
        on("vector", lambda v, vt2=vt2: red_tree(v, g_trash2[:], 4, vt2),
           waits=[("chain_s", s3)])
        s4 = kb.tks("scalar", 2)
        def r2fn(s, s4=s4):
            s.activation(out=tr_stat[:, 7:8], in_=tr_stat[:, 6:7], func=AF.Ln,
                         bias=ccol[:, 0:1]).then_inc(CS, 1)
            s.wait_ge(CS, s4[0])
            return s.activation(out=tr_stat[:, 7:8], in_=tr_stat[:, 7:8],
                                func=AF.Exp, scale=-0.5).then_inc(CS, 1)
        on("scalar", r2fn, waits=[("chain_v", vt2[7])])
        vv = kb.tks("vector", 1)
        on("vector", lambda v: v.tensor_scalar(
            out=g_trash[:], in0=g_trash2[:], scalar1=tr_stat[:, 4:5],
            scalar2=tr_stat[:, 7:8], op0=ALU.subtract,
            op1=ALU.mult).then_inc(CV, 1), waits=[("chain_s", s4[1]), txs])
        twl = [("chain_v", vv[0])]
        if t >= 1:
            twl.append(("chain_v", tprev[t - 1]["vdone"]))
        for c in range(8):
            on("tensor", lambda t_, c=c: t_.transpose(
                PB[0][:, c * 128:(c + 1) * 128] if c < 4 else PB[1][:, (c - 4) * 128:(c - 3) * 128],
                g_trash[:, c * 128:(c + 1) * 128], identf[:]),
               waits=twl if c == 0 else [], inc=(1 if c == 7 else 0))
        t_tr = kb.tok("tensor")
        vev = kb.tks("vector", 8)
        def curev(v, t=t):
            last = None
            for c in range(8):
                src = (PB[0][:, c * 128:(c + 1) * 128] if c < 4
                       else PB[1][:, (c - 4) * 128:(c - 3) * 128])
                last = v.tensor_copy(out=CUR[:, c * 4096 + t * 128:c * 4096 + t * 128 + 128],
                                     in_=src).then_inc(CV, 1)
            return last
        on("vector", curev, waits=[t_tr])
        tprev[t] = dict(red1=vt[1], xp=vx[1], vdone=vev[7], spill=txs)
    t_cur = fence("vector")
    kb.tap("cur", CUR[:, 0:32768], (128, 32768))
    if ffn_stop == "cur":
        return kb, dict(locals())

    d_xmid = nc.dram_tensor("d_xmid", [128, 32 * D], F32, kind="Internal").ap()
    tok_bcB = tok_bc[:].bitcast(BF16)
    f_stage = tok_bc[:, 0:512]
    b1row = tok_bc[0:1, 0:256]
    TQT = tok_bcB[:, 1024:1536]
    TKT = tok_bcB[:, 1536:2048]
    PbF = tok_bcB[:, 2048:2688]
    PTfF = tok_bcB[:, 2688:3328]
    sqbF = tok_bcB[:, 3328:3840]
    WQKV = WREGb[:, 0:12288]
    w1sb = [WREGb[:, 0:2048], WREGb[:, 2048:4096]]
    w2sb = [WREGb[:, 4096 + 1024 * j:4096 + 1024 * (j + 1)] for j in range(8)]
    wtmpB = wtmp[:].bitcast(BF16)
    evbF = wtmpB[:, 2048:2560]
    rnrowF = wtmpB[:, 2560:3072]
    scr = wtmp[:, 0:1024]
    QB = act_trash[:].bitcast(BF16)
    null_bcB = null_bc[:].bitcast(BF16)
    vtmB = [g_trash[:].bitcast(BF16), g_trash2[:].bitcast(BF16)]
    wosb = [xs[0][:].bitcast(BF16), xs[1][:].bitcast(BF16)]
    zbufB = [lmasks[:, 0:512], lmasks[:, 512:1024]]
    Z2N = Abf[:, 0:32768]
    HBB = Abf[:, 32768:65536]
    mstg = [g_trash[:, 0:512], g_trash[:, 512:1024]]
    z1buf = [g_trash2[:, 0:512], g_trash2[:, 512:1024]]

    def kslot(s, c):
        if s == 0:
            return null_bcB[:, c * 512:(c + 1) * 512]
        return u2n_t[c // 2][:, (c % 2) * 512:(c % 2) * 512 + 512]

    PS_NRM = PSALL[0:2, 2048:2560]
    PS_RNBC = PSALL[:, 2048:2560]
    PS_T1 = PSALL[:, 2560:3072]
    PS_TBL = PSALL[:, 2048:2560]
    PS_S = PSALL[:, 3072:3712]
    PS_PTB = PSBALL[:, 6144:6784]
    PS_PV = PSALL[0:64, 3584:3712]
    PS_WO = [PSALL[:, 3072:3584], PSALL[:, 3584:4096]]
    PS_VT = PSBALL[:, 4096:4608]
    PS_R1 = PSALL[0:1, 2048:2560]
    PS_R2 = PSALL[0:1, 2560:3072]
    PS_R2b = PSALL[0:1, 3072:3584]
    PS_B1 = PSALL[0:1, 2048:2304]
    PS_B1T = PSALL[:, 2304:2306]
    MU_BC = PSALL[:, 3072:3584]
    R_BC = PSALL[:, 3584:4096]

    CG2 = kb.csem["gpsimd"]
    fs = {"dbg_gate": [], "pbfree": [t_cur, t_cur, t_cur, t_cur], "bank4_v": t_cur,
          "bank5_v": None, "scr_g": None, "wo_pe": t_cur, "tbl_fold": None, "rope_g": None,
          "vev_v": None, "exp1_s": None, "ptf_g": None, "aof_g": None,
          "layer_ready": t_cur, "w2_pe": None, "hmul_g": {}}

    def emit_table(l, tb):
        for ri, (tabdst, fcol, qcol) in enumerate(((TQT, 0, 0), (TKT, 128, 1))):
            wl = [fs["layer_ready"]]
            if fs["bank4_v"] is not None:
                wl.append(("chain_v", fs["bank4_v"]) if isinstance(fs["bank4_v"], int)
                          else fs["bank4_v"])
            if fs.get("aof_g") is not None:
                wl.append(("chain_s", fs["aof_g"]))
            tl = dma("sync", "ftbl", f_stage[:],
                     cst["frope"][tb * 512:(tb + 1) * 512, fcol:fcol + 128]
                     .rearrange("(s p) c -> p s c", p=128), waits=wl)
            for j in range(4):
                on("tensor", lambda t, j=j: t.transpose(
                    PS_TBL[:, j * 128:(j + 1) * 128], f_stage[:, j * 128:(j + 1) * 128],
                    identf[:]), waits=([tl] + wl) if j == 0 else [],
                   inc=(1 if j == 3 else 0))
            tt = kb.tok("tensor")
            vr = kb.tks("vector", 1)
            on("vector", lambda v, tabdst=tabdst, qcol=qcol: v.tensor_scalar_mul(
                tabdst, PS_TBL[:, 0:512], qkf[:, qcol:qcol + 1]).then_inc(CV, 1),
               waits=[tt])
            fs["bank4_v"] = vr[0]
        fs["tbl_fold"] = fs["bank4_v"]

    def emit_zcast(l, tb, kc, dst, waits):
        t0 = tb * 512
        if kc < 4 or t0 > 0:
            off = 0 if kc < 4 else -1
            vr = kb.tks("vector", 1)
            on("vector", lambda v, kc=kc, t0=t0, off=off, dst=dst: v.tensor_copy(
                out=dst, in_=CUR[:, kc * 4096 + t0 + off:kc * 4096 + t0 + off + 512])
               .then_inc(CV, 1), waits=waits)
            return vr[0]
        vr = kb.tks("vector", 2)
        def fn(v, kc=kc, dst=dst, vr=vr):
            v.memset(dst[:, 0:1], 0.0).then_inc(CV, 1)
            return v.tensor_copy(out=dst[:, 1:512],
                                 in_=CUR[:, kc * 4096:kc * 4096 + 511]).then_inc(CV, 1)
        on("vector", fn, waits=waits)
        return vr[1]

    def rope_muls(t1ps, tab, vr8):
        def fn(v, t1ps=t1ps, tab=tab):
            for hs in (0, 64):
                b0, b1_ = hs, hs + 32
                v.tensor_mul(scr[b0:b0 + 32, 0:512], t1ps[b0:b0 + 32, :],
                             tab[0:32, :]).then_inc(CV, 1)
                v.tensor_mul(scr[b0:b0 + 32, 512:1024], t1ps[b1_:b1_ + 32, :],
                             tab[64:96, :]).then_inc(CV, 1)
                v.tensor_mul(scr[b1_:b1_ + 32, 0:512], t1ps[b1_:b1_ + 32, :],
                             tab[32:64, :]).then_inc(CV, 1)
                v.tensor_mul(scr[b1_:b1_ + 32, 512:1024], t1ps[b0:b0 + 32, :],
                             tab[96:128, :]).then_inc(CV, 1)
            return None
        return fn

    ROPE_COMB_ENG = "vector"

    def rope_combs(dst, dcol, vr8, gt4):
        def fn(g, dst=dst, dcol=dcol, vr8=vr8):
            if ROPE_COMB_ENG == "gpsimd":
                g.wait_ge(CV, vr8[7])
            sem = CG2 if ROPE_COMB_ENG == "gpsimd" else CV
            for hs in (0, 64):
                b0, b1_ = hs, hs + 32
                g.tensor_sub(dst[b0:b0 + 32, dcol:dcol + 512],
                             scr[b0:b0 + 32, 0:512],
                             scr[b0:b0 + 32, 512:1024]).then_inc(sem, 1)
                g.tensor_add(dst[b1_:b1_ + 32, dcol:dcol + 512],
                             scr[b1_:b1_ + 32, 0:512],
                             scr[b1_:b1_ + 32, 512:1024]).then_inc(sem, 1)
            return None
        return fn

    def emit_qkv_f(l, tb):
        zlast = fs["zlast"]
        for wv_i in range(3):
            mmtoks = []
            for kc in range(8):
                zwl = []
                if zlast[kc % 2] is not None:
                    zwl.append(zlast[kc % 2])
                elif wv_i == 0 and kc < 2:
                    zwl.append(fs["layer_ready"])
                zc = emit_zcast(l, tb, kc, zbufB[kc % 2], zwl)
                for mo4 in range(4):
                    mo = wv_i * 4 + mo4
                    mwl = []
                    if kc == 0:
                        mwl = [("chain_v", zc)]
                        mwl.append(("chain_v", fs["pbfree"][mo4])
                                   if isinstance(fs["pbfree"][mo4], int)
                                   else fs["pbfree"][mo4])
                    on("tensor", lambda t, mo=mo, kc=kc, mo4=mo4: t.matmul(
                        PB[mo4][:, 0:512], WQKV[:, kc * 1536 + mo * 128:kc * 1536 + mo * 128 + 128],
                        zbufB[kc % 2], start=(kc == 0), stop=(kc == 7)),
                       waits=mwl, inc=(1 if mo4 == 3 else 0))
                zlast[kc % 2] = kb.tok("tensor")
            twave = kb.tok("tensor")
            if wv_i < 2:
                for oc in range(4):
                    par = oc % 2
                    nrm_ = PS_NRM if par == 0 else PSALL[0:2, 3072:3584]
                    rnbc_ = PS_RNBC if par == 0 else PSALL[:, 3072:3584]
                    t1_ = PS_T1 if par == 0 else PSALL[:, 3584:4096]
                    rn_ = rnrowF if par == 0 else wtmpB[64:66, 2560:3072]
                    fr_ = f_stage[0:2, :] if par == 0 else f_stage[32:34, :]
                    vr = kb.tks("vector", 2)
                    evwl = [twave]
                    if fs.get("tnrm_prev") is not None:
                        evwl.append(fs["tnrm_prev"])
                    on("vector", lambda v, oc=oc, vr=vr: (
                        v.tensor_copy(out=evbF, in_=PB[oc][:, 0:512]).then_inc(CV, 1),
                        v.wait_ge(CV, vr[0]),
                        v.tensor_mul(sqbF, evbF, evbF).then_inc(CV, 1))[-1],
                       waits=evwl)
                    fs["pbfree"][oc] = vr[0]
                    nwl = [("chain_v", vr[1])]
                    if fs["nrmfree"][par] is not None:
                        nwl.append(("chain_v", fs["nrmfree"][par]))
                    if par == 1:
                        if fs.get("wo_v") is not None:
                            nwl.append(("chain_v", fs["wo_v"]))
                        if fs.get("aof_g") is not None:
                            nwl.append(("chain_s", fs["aof_g"]))
                    on("tensor", lambda t, nrm_=nrm_: t.matmul(
                        nrm_, bdones_sb[:], sqbF, start=True, stop=True,
                        skip_group_check=True),
                       waits=nwl, inc=1)
                    tnrm = kb.tok("tensor")
                    fs["tnrm_prev"] = tnrm
                    sr = kb.tks("scalar", 1)
                    on("scalar", lambda s, nrm_=nrm_, fr_=fr_: s.activation(
                        out=fr_, in_=nrm_, func=AF.Sqrt).then_inc(CS, 1),
                       waits=[tnrm])
                    vr2 = kb.tks("vector", 2)
                    def rnfn(v, vr2=vr2, fr_=fr_, rn_=rn_):
                        v.reciprocal(fr_, fr_).then_inc(CV, 1)
                        v.wait_ge(CV, vr2[0])
                        return v.tensor_scalar(
                            out=rn_[0:2, :], in0=fr_, scalar1=1.0e12,
                            scalar2=None, op0=ALU.min).then_inc(CV, 1)
                    on("vector", rnfn, waits=[("chain_s", sr[0])])
                    sel_l = sel2_sb[0:2, :] if par == 0 else sel2_sb[64:66, :]
                    on("tensor", lambda t, rnbc_=rnbc_, rn_=rn_, sel_l=sel_l: t.matmul(
                        rnbc_, sel_l, rn_[0:2, :], start=True, stop=True,
                        skip_group_check=True),
                       waits=[("chain_v", vr2[1])], inc=1)
                    tsel = kb.tok("tensor")
                    vr3 = kb.tks("vector", 1)
                    on("vector", lambda v, t1_=t1_, rnbc_=rnbc_: v.tensor_mul(
                        t1_, evbF, rnbc_).then_inc(CV, 1), waits=[tsel])
                    fs["nrmfree"][par] = vr3[0]
                    if wv_i == 0:
                        dst, dcol = QB, oc * 512
                        tab = TQT
                    else:
                        dst, dcol = kslot(tb % 2, oc), 0
                        tab = TKT
                    vr8 = kb.tks("vector", 8)
                    mwl2 = [("chain_v", fs["tbl_fold"])]
                    on("vector", rope_muls(t1_, tab, vr8), waits=mwl2)
                    gt4 = kb.tks(ROPE_COMB_ENG, 4)
                    gwl = []
                    if fs["wo_pe"] is not None:
                        gwl.append(fs["wo_pe"])
                    if ROPE_COMB_ENG == "vector":
                        gwl.append(("chain_v", vr8[7]))
                    on(ROPE_COMB_ENG, rope_combs(dst, dcol, vr8, gt4), waits=gwl)
                fs["rope_g"] = (("chain_g", kb.ccnt["gpsimd"])
                                if ROPE_COMB_ENG == "gpsimd"
                                else ("chain_v", kb.ccnt["vector"]))
            else:
                for oc in range(4):
                    vr = kb.tks("vector", 1)
                    on("vector", lambda v, oc=oc: v.tensor_copy(
                        out=evbF, in_=PB[oc][:, 0:512]).then_inc(CV, 1),
                       waits=[twave, ("tensor", kb.cnt["tensor"])])
                    fs["pbfree"][oc] = vr[0]
                    vtwl = [("chain_v", vr[0]), ("chain_v", fs["bank4_v"])]
                    for j in range(4):
                        on("tensor", lambda t, j=j: t.transpose(
                            PS_VT[:, j * 128:(j + 1) * 128],
                            evbF[:, j * 128:(j + 1) * 128], identb[:]),
                           waits=vtwl if j == 0 else [], inc=(1 if j == 3 else 0))
                    tvt = kb.tok("tensor")
                    vr4 = kb.tks("vector", 4)
                    def vev(v, oc=oc, tb=tb):
                        last = None
                        for j in range(4):
                            last = v.tensor_copy(
                                out=vtmB[tb % 2][:, j * 512 + oc * 128:j * 512 + oc * 128 + 128],
                                in_=PS_VT[:, j * 128:(j + 1) * 128]).then_inc(CV, 1)
                        return last
                    on("vector", vev, waits=[tvt])
                    fs["bank4_v"] = vr4[3]
                fs["vev_v"] = kb.ccnt["vector"]

    def emit_ffn_dbg(l, tb):
        if l != 0 or tb != 0:
            return
        specs = [("fq0", QB, (128, 2048), BF16), ("fk0", null_bcB, (128, 2048), BF16),
                 ("fv0", vtmB[0], (128, 2048), BF16), ("ftq", TQT, (128, 512), BF16),
                 ("ftk", TKT, (128, 512), BF16)]
        for nm, ap_, shp, dt in specs:
            if nm in kb.debug:
                t_ = nc.dram_tensor("tap_" + nm, list(shp), dt, kind="ExternalOutput").ap()
                kb.taps[nm] = t_
                tok = dma("sync", "dbgf", t_, ap_,
                          waits=[fs["rope_g"], ("chain_v", fs["vev_v"])])
                fs["dbg_gate"].append(tok)

    S_B = [PSALL[:, 3072:3712], PSALL[:, 2048:2688]]
    PTB_B = [PSBALL[:, 6144:6784], PSBALL[:, 4096:4736]]
    PV_B = [PSALL[0:64, 3584:3712], PSALL[0:64, 2560:2688]]
    Pb_B = [tok_bcB[:, 2048:2688], tok_bcB[:, 0:640]]

    def emit_attn_S(l, tb, o, h, par, first_of_tb):
        gt = 4 * tb + o
        lhs = QB[(h % 2) * 64:(h % 2) * 64 + 64,
                 (h // 2) * 512 + o * 128:(h // 2) * 512 + o * 128 + 128]
        ops = []
        if tb == 0:
            ops.append((0, 512, ("k", kslot(tb % 2, h // 2), 0, 512), True))
            moff = 512 - 128 * gt
            ops.append((0, 512, ("m", maskX_sb, moff, moff + 512), False))
            ops.append((512, 640, ("m", maskX_sb, moff + 512, moff + 640), True))
        else:
            w1_ = 512 - 128 * o
            ops.append((0, w1_, ("k", kslot((tb - 1) % 2, h // 2), o * 128, 512), True))
            if o > 0:
                ops.append((w1_, 512, ("k", kslot(tb % 2, h // 2), 0, 128 * o), False))
            ops.append((512, 640, ("k", kslot(tb % 2, h // 2), 128 * o, 128 * o + 128), True))
            ops.append((0, 512, ("m", maskX_sb, 0, 512), False))
            ops.append((512, 640, ("m", maskX_sb, 512, 640), False))
        Sps = S_B[par]
        first = True
        nops = len(ops)
        for oi, (c0, c1, srcspec, st_) in enumerate(ops):
            wl = []
            if first:
                wl = list(fs["dbg_gate"])
                fs["dbg_gate"] = []
                if first_of_tb:
                    wl += [fs["rope_g"], ("chain_v", fs["vev_v"])]
                    if fs.get("wo_v") is not None:
                        wl.append(("chain_v", fs["wo_v"]))
                if fs["exp1_p"][par] is not None:
                    wl.append(("chain_s", fs["exp1_p"][par]))
                    wl.append(("chain_s", fs["ptf_p"][par]))
                    wl.append(("chain_s", fs["aof_p"][par]))
                first = False
            kind, srcb, s0, s1 = srcspec
            if kind == "k":
                rhs = srcb[(h % 2) * 64:(h % 2) * 64 + 64, s0:s1]
                lh = lhs
            else:
                rhs = srcb[:, s0:s1]
                lh = identb[:]
            on("tensor", lambda t, c0=c0, c1=c1, rhs=rhs, lh=lh, st_=st_, Sps=Sps:
               t.matmul(Sps[:, c0:c1], lh, rhs, start=st_, stop=False,
                        skip_group_check=True),
               waits=wl, inc=(1 if oi == nops - 1 else 0))
        return kb.tok("tensor")

    def emit_attn_rest(l, tb, o, h, par, tS):
        gt = 4 * tb + o
        lhs = QB[(h % 2) * 64:(h % 2) * 64 + 64,
                 (h // 2) * 512 + o * 128:(h // 2) * 512 + o * 128 + 128]
        Sps, Pb, PTB, PVp = S_B[par], Pb_B[par], PTB_B[par], PV_B[par]
        sc = 8 + 3 * par
        exwl = []
        if l == 0 and tb == 0 and o == 0 and h == 0 and "fS0" in kb.debug:
            t_ = nc.dram_tensor("tap_fS0", [128, 640], BF16, kind="ExternalOutput").ap()
            kb.taps["fS0"] = t_
            vrd = kb.tks("vector", 1)
            on("vector", lambda v: v.tensor_copy(out=Pb, in_=Sps).then_inc(CV, 1),
               waits=[tS])
            exwl.append(dma("sync", "dbgf", t_, Pb, waits=[("chain_v", vrd[0])]))
        vr = kb.tks("vector", 1)
        on("vector", lambda v: v.reduce_max(out=stat[:, sc:sc + 1], in_=Sps,
                                            axis=AX.X, negate=True).then_inc(CV, 1),
           waits=[tS])
        sr = kb.tks("scalar", 1)
        on("scalar", lambda s: s.activation(out=Pb, in_=Sps, func=AF.Exp,
                                            bias=stat[:, sc:sc + 1]).then_inc(CS, 1),
           waits=[("chain_v", vr[0])] + exwl)
        fs["exp1_p"][par] = sr[0]
        vr2 = kb.tks("vector", 2)
        def sumfn(v, vr2=vr2):
            v.reduce_sum(out=stat[:, sc + 1:sc + 2], in_=Pb, axis=AX.X).then_inc(CV, 1)
            v.wait_ge(CV, vr2[0])
            return v.reciprocal(stat[:, sc + 2:sc + 3],
                                stat[:, sc + 1:sc + 2]).then_inc(CV, 1)
        on("vector", sumfn, waits=[("chain_s", sr[0])])
        sr2 = kb.tks("scalar", 1)
        on("scalar", lambda s: s.activation(out=Pb, in_=Pb, func=AF.Identity,
                                            scale=stat[:, sc + 2:sc + 3]).then_inc(CS, 1),
           waits=[("chain_v", vr2[1])])
        if l == 0 and tb == 0 and o == 0 and h == 0 and "fP0" in kb.debug:
            t_ = nc.dram_tensor("tap_fP0", [128, 640], BF16, kind="ExternalOutput").ap()
            kb.taps["fP0"] = t_
            fs["dbg_gate"].append(dma("sync", "dbgf", t_, Pb,
                                      waits=[("chain_s", sr2[0])]))
        for b in range(5):
            on("tensor", lambda t, b=b, Pb=Pb, PTB=PTB: t.transpose(
                PTB[:, b * 128:(b + 1) * 128], Pb[:, b * 128:(b + 1) * 128],
                identb[:]), waits=[("chain_s", sr2[0])] if b == 0 else [],
               inc=(1 if b == 4 else 0))
        tPT = kb.tok("tensor")
        srp = kb.tks("scalar", 1)
        on("scalar", lambda s, PTB=PTB: s.activation(out=PTfF, in_=PTB,
                                                     func=AF.Identity).then_inc(CS, 1),
           waits=[tPT])
        fs["ptf_p"][par] = srp[0]
        sts = [st for st in range(gt - 4, gt + 1) if st >= 0]
        for pi, st in enumerate(sts):
            cidx = st - (gt - 4) if gt >= 4 else st
            vsl = vtmB[(st // 4) % 2][:, (st % 4) * 512 + h * 64:
                                      (st % 4) * 512 + h * 64 + 64]
            on("tensor", lambda t, vsl=vsl, cidx=cidx, pi=pi, np_=len(sts), PVp=PVp:
               t.matmul(PVp, vsl, PTfF[:, cidx * 128:(cidx + 1) * 128],
                        start=(pi == 0), stop=(pi == np_ - 1), skip_group_check=True),
               waits=[("chain_s", srp[0])] if pi == 0 else [],
               inc=(1 if pi == len(sts) - 1 else 0))
        tPV = kb.tok("tensor")
        sra = kb.tks("scalar", 1)
        on("scalar", lambda s, lhs=lhs, PVp=PVp: s.activation(
            out=lhs, in_=PVp, func=AF.Identity).then_inc(CS, 1), waits=[tPV])
        fs["aof_p"][par] = sra[0]
        fs["aof_g"] = sra[0]

    def emit_attn_f(l, tb):
        pend = None
        for i in range(32):
            o, h = i // 8, i % 8
            tS = emit_attn_S(l, tb, o, h, i % 2, first_of_tb=(i == 0))
            if pend is not None:
                emit_attn_rest(l, tb, *pend)
            pend = (o, h, i % 2, tS)
        emit_attn_rest(l, tb, *pend)

    def emit_wo_f(l, tb):
        woev = {}
        dbg_toks = []
        if l == 0 and tb == 0 and "fo0" in kb.debug:
            t_ = nc.dram_tensor("tap_fo0", [128, 2048], BF16, kind="ExternalOutput").ap()
            kb.taps["fo0"] = t_
            dbg_toks.append(dma("sync", "dbgf", t_, QB,
                                waits=[("chain_s", fs["aof_g"])]))
        for mD in range(8):
            pw = PS_WO[mD % 2]
            for kc in range(4):
                wl = []
                if kc == 0:
                    wl = [("chain_s", fs["aof_g"])] + dbg_toks
                    if mD >= 2:
                        wl.append(("chain_v", woev[mD - 2]))
                on("tensor", lambda t, mD=mD, kc=kc, pw=pw: t.matmul(
                    pw[:, 0:512], wosb[kc // 2][:, (kc % 2) * 1024 + mD * 128:
                                                (kc % 2) * 1024 + mD * 128 + 128],
                    QB[:, kc * 512:(kc + 1) * 512], start=(kc == 0), stop=(kc == 3),
                    skip_group_check=True),
                   waits=wl, inc=(1 if kc == 3 else 0))
            tmm = kb.tok("tensor")
            vr = kb.tks("vector", 1)
            on("vector", lambda v, mD=mD, tb=tb, pw=pw: v.tensor_add(
                CUR[:, mD * 4096 + tb * 512:mD * 4096 + tb * 512 + 512],
                CUR[:, mD * 4096 + tb * 512:mD * 4096 + tb * 512 + 512],
                pw[:, 0:512]).then_inc(CV, 1), waits=[tmm])
            woev[mD] = vr[0]
        fs["wo_pe"] = kb.tok("tensor")
        fs["wo_v"] = kb.ccnt["vector"]

    def emit_layer_setup(l):
        wl0 = [fs["layer_ready"]]
        t_pv = None
        for name, c0 in (("f_lg", 0), ("f_lb", 8), ("f_b2", 16)):
            tl = dma("sync", "fvec", bt8[:],
                     w[name][l].rearrange("(c p) -> c p", p=128),
                     waits=[t_pv] if t_pv else wl0)
            tt = on("tensor", lambda t, c0=c0: t.transpose(
                PB[5][:, c0:c0 + 8], bt8[:8, :], identf[:8, :8]),
                waits=[tl] + wl0, inc=1)
            t_pv = tt
            vr = kb.tks("vector", 1)
            on("vector", lambda v, c0=c0: v.tensor_copy(
                out=lgb[:, c0:c0 + 8], in_=PB[5][:, c0:c0 + 8]).then_inc(CV, 1),
                waits=[tt])
            t_pv = fence("vector")
        vr = kb.tks("vector", 1)
        on("vector", lambda v: v.tensor_copy(out=lbb[:], in_=lgb[:, 8:16])
           .then_inc(CV, 1), waits=[t_pv])
        for name, qcol in (("f_qs", 0), ("f_ks", 1)):
            tl = dma("sync", "fvec", srow[0:1, 0:64], w[name][l].unsqueeze(0),
                     waits=[t_pv])
            tt = on("tensor", lambda t: t.transpose(
                PB[5][0:64, 8:9], srow[0:1, 0:64], identf[0:1, 0:1]),
                waits=[tl], inc=1)
            qsc = 8.0 if qcol == 0 else 1.0
            vr = kb.tks("vector", 3)
            def qf_fn(v, qcol=qcol, qsc=qsc, vr=vr):
                v.tensor_scalar_mul(qkf[0:64, qcol:qcol + 1],
                                    PB[5][0:64, 8:9], qsc).then_inc(CV, 1)
                v.tensor_scalar_mul(qkf[64:96, qcol:qcol + 1],
                                    PB[5][32:64, 8:9], qsc).then_inc(CV, 1)
                return v.tensor_scalar_mul(qkf[96:128, qcol:qcol + 1],
                                           PB[5][0:32, 8:9], qsc).then_inc(CV, 1)
            on("vector", qf_fn, waits=[tt])
            t_pv = fence("vector")
        t_wc = t_pv
        for kc in range(8):
            tl = dma("sync", "fwl", wtmp[:, 0:1536],
                     w["f_wqkv"][l][kc * 128:(kc + 1) * 128, :], waits=[t_wc] + wl0)
            vr = kb.tks("vector", 1)
            on("vector", lambda v, kc=kc: v.tensor_copy(
                out=WQKV[:, kc * 1536:(kc + 1) * 1536],
                in_=wtmp[:, 0:1536]).then_inc(CV, 1), waits=[tl])
            t_wc = fence("vector")
        for kc in range(4):
            tl = dma("sync", "fwl", wtmp[:, 0:1024],
                     w["f_wo"][l][kc * 128:(kc + 1) * 128, :], waits=[t_wc])
            vr = kb.tks("vector", 1)
            on("vector", lambda v, kc=kc: v.tensor_copy(
                out=wosb[kc // 2][:, (kc % 2) * 1024:(kc % 2) * 1024 + 1024],
                in_=wtmp[:, 0:1024]).then_inc(CV, 1), waits=[tl])
            t_wc = fence("vector")
        return t_wc

    def emit_spill(l):
        t_all = fence("vector")
        last = None
        for c in range(8):
            last = dma("sync", "xms", d_xmid[:, c * 4096:(c + 1) * 4096],
                       CUR[:, c * 4096:(c + 1) * 4096], waits=[t_all])
        return last

    def emit_z2n(l, t_spill_x):
        gi = 0
        mlast = {0: None, 1: None}
        for tb in range(8):
            t0 = tb * 512
            casts = []
            for kc in range(8):
                wl = [t_spill_x]
                if mlast[gi % 2] is not None:
                    wl.append(("chain_v", mlast[gi % 2]))
                dst = Z2N[:, kc * 4096 + t0:kc * 4096 + t0 + 512]
                if kc < 4:
                    tl = dma("sync", f"zx{gi % 2}", mstg[gi % 2],
                             d_xmid[:, kc * 4096 + t0:kc * 4096 + t0 + 512], waits=wl)
                    vr = kb.tks("vector", 1)
                    on("vector", lambda v, dst=dst, b=gi % 2: v.tensor_copy(
                        out=dst, in_=mstg[b]).then_inc(CV, 1), waits=[tl])
                    ctok = vr[0]
                elif t0 > 0:
                    tl = dma("sync", f"zx{gi % 2}", mstg[gi % 2],
                             d_xmid[:, kc * 4096 + t0 - 1:kc * 4096 + t0 + 511],
                             waits=wl)
                    vr = kb.tks("vector", 1)
                    on("vector", lambda v, dst=dst, b=gi % 2: v.tensor_copy(
                        out=dst, in_=mstg[b]).then_inc(CV, 1), waits=[tl])
                    ctok = vr[0]
                else:
                    tl = dma("sync", f"zx{gi % 2}", mstg[gi % 2][:, 0:511],
                             d_xmid[:, kc * 4096:kc * 4096 + 511], waits=wl)
                    vr = kb.tks("vector", 2)
                    def cfn(v, dst=dst, b=gi % 2, vr=vr):
                        v.memset(dst[:, 0:1], 0.0).then_inc(CV, 1)
                        return v.tensor_copy(out=dst[:, 1:512],
                                             in_=mstg[b][:, 0:511]).then_inc(CV, 1)
                    on("vector", cfn, waits=[tl])
                    ctok = vr[1]
                mlast[gi % 2] = ctok
                gi += 1
                gr = kb.tks("vector", 1)
                sqwl = [("chain_v", ctok), ("tensor", kb.cnt["tensor"])]
                rwl = [("chain_v", ctok)]
                if kc == 0 and fs.get("rows_v") is not None:
                    rwl.append(("chain_v", fs["rows_v"]))
                on("vector", lambda v, dst=dst: v.tensor_mul(
                    sqbF, dst, dst).then_inc(CV, 1), waits=sqwl)
                on("tensor", lambda t, dst=dst, kc=kc: t.matmul(
                    PS_R1, onesbcol[:], dst, start=(kc == 0), stop=(kc == 7),
                    skip_group_check=True),
                   waits=rwl, inc=0)
                on("tensor", lambda t, kc=kc: t.matmul(
                    PS_R2, onesbcol[:], sqbF, start=(kc == 0), stop=(kc == 7),
                    skip_group_check=True),
                   waits=[("chain_v", gr[0])], inc=1)
                casts.append(ctok)
            trows = kb.tok("tensor")
            vr = kb.tks("vector", 4)
            def rowfn(v, vr=vr):
                v.tensor_scalar_mul(scr[0:1, 0:512], PS_R1, 1.0 / D).then_inc(CV, 1)
                v.tensor_scalar_mul(scr[0:1, 512:1024], PS_R2, 1.0 / D).then_inc(CV, 1)
                v.wait_ge(CV, vr[1])
                v.tensor_mul(PS_R2b, scr[0:1, 0:512], scr[0:1, 0:512]).then_inc(CV, 1)
                v.wait_ge(CV, vr[2])
                return v.tensor_sub(PS_R2, scr[0:1, 512:1024], PS_R2b).then_inc(CV, 1)
            on("vector", rowfn, waits=[trows])
            sr = kb.tks("scalar", 2)
            def zrfn(s, sr=sr):
                s.activation(out=scr[0:1, 512:1024], in_=PS_R2, func=AF.Ln,
                             bias=ccol[0:1, 0:1]).then_inc(CS, 1)
                s.wait_ge(CS, sr[0])
                return s.activation(out=scr[0:1, 512:1024], in_=scr[0:1, 512:1024],
                                    func=AF.Exp, scale=-0.5).then_inc(CS, 1)
            on("scalar", zrfn, waits=[("chain_v", vr[3])])
            vr2 = kb.tks("vector", 2)
            def rowcp(v, vr2=vr2):
                v.tensor_copy(out=rnrowF[0:1, :], in_=scr[0:1, 0:512]).then_inc(CV, 1)
                return v.tensor_copy(out=evbF[0:1, :],
                                     in_=scr[0:1, 512:1024]).then_inc(CV, 1)
            on("vector", rowcp, waits=[("chain_s", sr[1])])
            fs["rows_v"] = vr2[1]
            bcwl = [("chain_v", vr2[1])]
            if fs.get("aff_v") is not None:
                bcwl.append(("chain_v", fs["aff_v"]))
            on("tensor", lambda t: t.matmul(MU_BC, onesb1x[:1, :], rnrowF[0:1, :],
                                            start=True, stop=True,
                                            skip_group_check=True),
               waits=bcwl, inc=0)
            on("tensor", lambda t: t.matmul(R_BC, onesb1x[:1, :], evbF[0:1, :],
                                            start=True, stop=True,
                                            skip_group_check=True),
               waits=[], inc=1)
            tbc = kb.tok("tensor")
            for kc in range(8):
                dst = Z2N[:, kc * 4096 + t0:kc * 4096 + t0 + 512]
                vr3 = kb.tks("vector", 2)
                def affn(v, dst=dst, vr3=vr3):
                    v.tensor_sub(scr[:, 0:512], dst, MU_BC).then_inc(CV, 1)
                    v.wait_ge(CV, vr3[0])
                    return v.tensor_mul(dst, scr[:, 0:512], R_BC).then_inc(CV, 1)
                on("vector", affn, waits=[tbc])
            fs["aff_v"] = kb.ccnt["vector"]
        return fence("vector")

    def emit_w1group(l, jg0, glen, t_z2n):
        for ji, j in enumerate(range(jg0, jg0 + glen)):
            wsb = w1sb[j % 2]
            wl = [t_z2n]
            if fs["w2_pe"] is not None:
                wl.append(fs["w2_pe"])
            if (j - 2) in fs["w1pe"]:
                wl.append(fs["w1pe"][j - 2])
            t_wc = None
            for piece in range(2):
                tl = dma("sync", "fw1", wtmp[:, 0:1024],
                         w["f_w1p"][l, j][:, piece * 1024:(piece + 1) * 1024],
                         waits=(wl if piece == 0 else [t_wc]))
                vr = kb.tks("vector", 1)
                on("vector", lambda v, piece=piece, wsb=wsb: v.tensor_copy(
                    out=wsb[:, piece * 1024:(piece + 1) * 1024],
                    in_=wtmp[:, 0:1024]).then_inc(CV, 1), waits=[tl])
                t_wc = fence("vector")
            for kc in range(8):
                on("tensor", lambda t, kc=kc, wsb=wsb: t.matmul(
                    PS_B1, lbb[:, kc:kc + 1], wsb[:, kc * 256:(kc + 1) * 256],
                    start=(kc == 0), stop=(kc == 7), skip_group_check=True),
                   waits=[t_wc] if kc == 0 else [], inc=(1 if kc == 7 else 0))
            tb1 = kb.tok("tensor")
            tlb = dma("sync", "fb1", b1row, w["f_b1p"][l, j].unsqueeze(0),
                      waits=[tb1])
            vr = kb.tks("vector", 1)
            on("vector", lambda v: v.tensor_add(b1row, b1row, PS_B1).then_inc(CV, 1),
               waits=[tlb, tb1])
            tvb = fence("vector")
            for bi in range(2):
                on("tensor", lambda t, bi=bi: t.transpose(
                    PS_B1T[:, bi:bi + 1], b1row[0:1, bi * 128:(bi + 1) * 128],
                    identf[0:1, 0:1]), waits=[tvb] if bi == 0 else [],
                   inc=(1 if bi == 1 else 0))
            tb1t = kb.tok("tensor")
            vr = kb.tks("vector", 9)
            fwl = [tb1t]
            if fs.get("gel_s") is not None:
                fwl.append(("chain_s", fs["gel_s"]))
            def foldfn(v, wsb=wsb, vr=vr):
                v.tensor_copy(out=b1c[:], in_=PS_B1T).then_inc(CV, 1)
                last = None
                for kc in range(8):
                    last = v.tensor_scalar_mul(wsb[:, kc * 256:(kc + 1) * 256],
                                               wsb[:, kc * 256:(kc + 1) * 256],
                                               lgb[:, kc:kc + 1]).then_inc(CV, 1)
                return last
            on("vector", foldfn, waits=fwl)
            t_fold = fence("vector")
            for tb in range(8):
                par = tb % 2
                pa, pg = (PB[0], PB[1]) if par == 0 else (PB[2], PB[3])
                ev_ = evbF if par == 0 else tok_bcB[:, 2048:2560]
                sq_ = sqbF if par == 0 else tok_bcB[:, 2560:3072]
                for kc in range(8):
                    wl2 = []
                    if kc == 0:
                        wl2 = [t_fold]
                        if fs["hprev_p"][par] is not None:
                            wl2.append(("chain_v", fs["hprev_p"][par]))
                        if fs.get("w2ev_v") is not None:
                            wl2.append(("chain_v", fs["w2ev_v"]))
                    on("tensor", lambda t, kc=kc, tb=tb, wsb=wsb, pa=pa: t.matmul(
                        pa[:, 0:512], wsb[:, kc * 256:kc * 256 + 128],
                        Z2N[:, kc * 4096 + tb * 512:kc * 4096 + tb * 512 + 512],
                        start=(kc == 0), stop=(kc == 7)), waits=wl2, inc=0)
                    on("tensor", lambda t, kc=kc, tb=tb, wsb=wsb, pg=pg: t.matmul(
                        pg[:, 0:512], wsb[:, kc * 256 + 128:kc * 256 + 256],
                        Z2N[:, kc * 4096 + tb * 512:kc * 4096 + tb * 512 + 512],
                        start=(kc == 0), stop=(kc == 7)), waits=[],
                       inc=(1 if kc == 7 else 0))
                tag = kb.tok("tensor")
                sr = kb.tks("scalar", 2)
                def gelfn(s, sr=sr, pa=pa, pg=pg, ev_=ev_, sq_=sq_):
                    s.activation(out=sq_, in_=pg[:, 0:512],
                                 func=AF.Gelu, bias=b1c[:, 1:2]).then_inc(CS, 1)
                    return s.activation(out=ev_, in_=pa[:, 0:512],
                                        func=AF.Identity,
                                        bias=b1c[:, 0:1]).then_inc(CS, 1)
                gwl2 = [tag]
                if fs["hprev_p"][par] is not None:
                    gwl2.append(("chain_v", fs["hprev_p"][par]))
                on("scalar", gelfn, waits=gwl2)
                fs["gel_s"] = sr[1]
                gr = kb.tks("vector", 1)
                on("vector", lambda v, ji=ji, tb=tb, ev_=ev_, sq_=sq_: v.tensor_mul(
                    HBB[:, ji * 4096 + tb * 512:ji * 4096 + tb * 512 + 512],
                    ev_, sq_).then_inc(CV, 1),
                   waits=[("chain_s", sr[1]), ("tensor", kb.cnt["tensor"])])
                fs["hprev_p"][par] = gr[0]
                fs["hprev"] = gr[0]
            fs["w1pe"][j] = kb.tok("tensor")

    def emit_w2group(l, jg0, glen, first_group):
        rows = [min(128, 2730 - j * 128) for j in range(jg0, jg0 + glen)]
        t_wc = None
        for ji, j in enumerate(range(jg0, jg0 + glen)):
            wl = [("chain_v", fs["hprev"])] if t_wc is None else [t_wc]
            tl = dma("sync", "fw2", wtmp[0:rows[ji], 0:1024],
                     w["f_w2"][l][j * 128:j * 128 + rows[ji], :], waits=wl)
            vr = kb.tks("vector", 1)
            on("vector", lambda v, ji=ji, r=rows[ji]: v.tensor_copy(
                out=w2sb[ji][0:r, :], in_=wtmp[0:r, 0:1024]).then_inc(CV, 1),
                waits=[tl])
            t_wc = fence("vector")
        zlast = {0: None, 1: None}
        for mD in range(8):
            for tb in range(8):
                alt = (mD * 8 + tb) % 2
                pw = PB[2 + alt]
                for ji in range(glen):
                    wl2 = []
                    if ji == 0:
                        wl2 = [t_wc, ("chain_v", fs["hprev"])]
                        if zlast[alt] is not None:
                            wl2.append(zlast[alt])
                    on("tensor", lambda t, ji=ji, mD=mD, tb=tb, pw=pw, r=rows[ji]: t.matmul(
                        pw[:, 0:512], w2sb[ji][0:r, mD * 128:(mD + 1) * 128],
                        HBB[0:r, ji * 4096 + tb * 512:ji * 4096 + tb * 512 + 512],
                        start=(ji == 0), stop=(ji == glen - 1)),
                       waits=wl2, inc=(1 if ji == glen - 1 else 0))
                tmm = kb.tok("tensor")
                vr = kb.tks("vector", 1)
                if first_group:
                    on("vector", lambda v, alt=alt, mD=mD, pw=pw: v.tensor_scalar(
                        out=z1buf[alt], in0=pw[:, 0:512],
                        scalar1=lgb[:, 16 + mD:17 + mD], scalar2=None,
                        op0=ALU.add).then_inc(CV, 1), waits=[tmm])
                else:
                    on("vector", lambda v, alt=alt, pw=pw: v.tensor_copy(
                        out=z1buf[alt], in_=pw[:, 0:512]).then_inc(CV, 1),
                       waits=[tmm])
                zlast[alt] = dma("gpsimd", f"w2a{alt}",
                                 d_xmid[:, mD * 4096 + tb * 512:mD * 4096 + tb * 512 + 512],
                                 z1buf[alt], waits=[("chain_v", vr[0])],
                                 accum_op=ALU.add)
        fs["w2_pe"] = kb.tok("tensor")
        fs["w2ev_v"] = kb.ccnt["vector"]
        return zlast

    def emit_reload(l):
        wl = [("ch", "w2a0", kb.chan.get("w2a0", 0)),
              ("ch", "w2a1", kb.chan.get("w2a1", 0)), ("tensor", fs["w2_pe"][1])]
        last = None
        for c in range(8):
            last = dma("sync", "xrl", CUR[:, c * 4096:(c + 1) * 4096],
                       d_xmid[:, c * 4096:(c + 1) * 4096], waits=wl)
        vr = kb.tks("vector", 1)
        on("vector", lambda v: v.memset(kb.fb[:1, 3:4], 0.0).then_inc(CV, 1),
           waits=[("ch", "xrl", kb.chan["xrl"])])
        return fence("vector")

    for l in range(2):
        t_setup = emit_layer_setup(l)
        fs["layer_ready"] = t_setup
        fs["exp1_p"] = {0: None, 1: None}
        fs["ptf_p"] = {0: None, 1: None}
        fs["aof_p"] = {0: None, 1: None}
        fs["wo_pe"] = fs["layer_ready"]
        fs["wo_v"] = None
        fs["bank4_v"] = kb.ccnt["vector"]
        fs["bank5_v"] = None
        fs["scr_g"] = None
        fs["nrmfree"] = {0: kb.ccnt["vector"], 1: kb.ccnt["vector"]}
        fs["tnrm_prev"] = None
        fs["w1pe"] = {}
        fs["gel_s"] = None
        fs["rows_v"] = None
        fs["aff_v"] = None
        fs["zlast"] = {0: None, 1: None}
        fs["pbfree"] = [t_setup] * 4
        for tb in range(8):
            emit_table(l, tb)
            emit_qkv_f(l, tb)
            emit_ffn_dbg(l, tb)
            emit_attn_f(l, tb)
            emit_wo_f(l, tb)
        if ffn_stop == "l1a" and l == 0:
            t_all = fence("vector")
            kb.tap("cura", CUR[:, 0:32768], (128, 32768))
            return kb, dict(locals())
        t_spill_x = emit_spill(l)
        fs["hprev"] = None
        fs["hprev_p"] = {0: None, 1: None}
        fs["w2_pe"] = None
        fs["w2ev_v"] = None
        t_z2n = emit_z2n(l, t_spill_x)
        for gidx, jg0 in enumerate((0, 8, 16)):
            glen = min(8, 22 - jg0)
            emit_w1group(l, jg0, glen, t_z2n)
            emit_w2group(l, jg0, glen, first_group=(gidx == 0))
        t_rel = emit_reload(l)
        fs["layer_ready"] = t_rel
        if ffn_stop == "l1" and l == 0:
            kb.tap("curl1", CUR[:, 0:32768], (128, 32768))
            return kb, dict(locals())

    fprev = {}
    for t in range(32):
        wl = [fs["layer_ready"]]
        if t >= 2:
            wl.append(("chain_v", fprev[t - 2]["out"]))
        tx = dma("sync", f"xfl{t % 2}", xs[t % 2][:],
                 d_xspill[t * 128:(t + 1) * 128, :], waits=wl)
        twl = list(wl)
        if t >= 1:
            twl.append(("chain_v", fprev[t - 1]["fev"]))
        for c in range(8):
            on("tensor", lambda t_, c=c, t=t: t_.transpose(
                PB[0][:, c * 128:(c + 1) * 128] if c < 4
                else PB[1][:, (c - 4) * 128:(c - 3) * 128],
                CUR[:, c * 4096 + t * 128:c * 4096 + t * 128 + 128], identf[:]),
               waits=twl if c == 0 else [], inc=(1 if c == 7 else 0))
        t_tr = kb.tok("tensor")
        vev = kb.tks("vector", 8)
        def fev(v, t=t):
            last = None
            for c in range(8):
                src = (PB[0][:, c * 128:(c + 1) * 128] if c < 4
                       else PB[1][:, (c - 4) * 128:(c - 3) * 128])
                last = v.tensor_copy(out=g_trash[:, c * 128:(c + 1) * 128],
                                     in_=src).then_inc(CV, 1)
            return last
        on("vector", fev, waits=[t_tr])
        swl = [("chain_v", vev[7])]
        if t >= 1:
            swl.append(("chain_v", fprev[t - 1]["red"]))
        s1, = kb.tks("scalar", 1)
        on("scalar", lambda s: s.activation(out=act_trash[:], in_=g_trash[:],
                                            func=AF.Square).then_inc(CS, 1),
           waits=swl)
        vt = kb.tks("vector", 8)
        on("vector", lambda v, vt=vt: red_tree(v, g_trash[:], 0, vt),
           waits=[("chain_s", s1)])
        s2 = kb.tks("scalar", 2)
        def rffn(s, s2=s2):
            s.activation(out=tr_stat[:, 3:4], in_=tr_stat[:, 2:3], func=AF.Ln,
                         bias=ccol[:, 0:1]).then_inc(CS, 1)
            s.wait_ge(CS, s2[0])
            return s.activation(out=tr_stat[:, 3:4], in_=tr_stat[:, 3:4],
                                func=AF.Exp, scale=-0.5).then_inc(CS, 1)
        on("scalar", rffn, waits=[("chain_v", vt[7])])
        vo = kb.tks("vector", 2)
        owl = [("chain_s", s2[1]), tx]
        if t >= 1:
            owl.append(("chain_v", fprev[t - 1]["out"]))
            owl.append(fprev[t - 1]["odma"])
        def outfn(v, t=t, vo=vo):
            v.tensor_scalar(out=g_trash2[:], in0=g_trash[:],
                            scalar1=tr_stat[:, 0:1], scalar2=tr_stat[:, 3:4],
                            op0=ALU.subtract, op1=ALU.mult).then_inc(CV, 1)
            v.wait_ge(CV, vo[0])
            return v.tensor_add(g_trash2[:], g_trash2[:], xs[t % 2][:]).then_inc(CV, 1)
        on("vector", outfn, waits=owl)
        todma = dma("sync", f"outw{t % 2}", out_h[t * 128:(t + 1) * 128, :],
                    g_trash2[:], waits=[("chain_v", vo[1])])
        fprev[t] = dict(fev=vev[7], red=vt[1], out=vo[1], odma=todma)

    return kb, dict(locals())



def make_in_maps(inputs):
    C = host_constants()
    X = np.ascontiguousarray(inputs["L0_attn_x"])
    m = {
        "w_ln_gamma": inputs["L0_attn_ln_gamma"], "w_ln_beta": inputs["L0_attn_ln_beta"],
        "w_light_qkv_w": inputs["L0_attn_light_qkv_w"], "w_light_out_w": inputs["L0_attn_light_out_w"],
        "w_q_route_tok": inputs["L0_attn_q_route_tok"], "w_kv_route_tok": inputs["L0_attn_kv_route_tok"],
        "w_heavy_gamma": inputs["L0_attn_heavy_gamma"], "w_heavy_q_w": inputs["L0_attn_heavy_q_w"],
        "w_heavy_kv_w": inputs["L0_attn_heavy_kv_w"], "w_heavy_null_kv": inputs["L0_attn_heavy_null_kv"],
        "w_heavy_out_w": inputs["L0_attn_heavy_out_w"], "w_null_q_token": inputs["L0_attn_null_q_token"],
        "w_f_wqkv": inputs["L0_ffn_wqkv"], "w_f_qs": inputs["L0_ffn_q_scale"],
        "w_f_ks": inputs["L0_ffn_k_scale"], "w_f_wo": inputs["L0_ffn_wo"],
        "w_f_lg": inputs["L0_ffn_ln_g"], "w_f_lb": inputs["L0_ffn_ln_b"],
        "w_f_w2": inputs["L0_ffn_w2"], "w_f_b2": inputs["L0_ffn_b2"],
    }
    w1 = np.asarray(inputs["L0_ffn_w1"], np.float32)
    b1 = np.asarray(inputs["L0_ffn_b1"], np.float32)
    HID = 2730
    w1p = np.zeros((2, 22, 128, 2048), np.float32)
    b1p = np.zeros((2, 22, 256), np.float32)
    for j in range(22):
        wdt = min(128, HID - j * 128)
        acols = slice(j * 128, j * 128 + wdt)
        gcols = slice(HID + j * 128, HID + j * 128 + wdt)
        wr = w1.reshape(2, 8, 128, 5460)
        w1p[:, j, :, :].reshape(2, 128, 8, 256)[:, :, :, 0:wdt] = \
            wr[:, :, :, acols].transpose(0, 2, 1, 3)
        w1p[:, j, :, :].reshape(2, 128, 8, 256)[:, :, :, 128:128 + wdt] = \
            wr[:, :, :, gcols].transpose(0, 2, 1, 3)
        b1p[:, j, 0:wdt] = b1[:, acols]
        b1p[:, j, 128:128 + wdt] = b1[:, gcols]
    m["w_f_w1p"] = w1p
    m["w_f_b1p"] = b1p
    m = {k: np.ascontiguousarray(v) for k, v in m.items()}
    for k, v in C.items():
        m["c_" + k] = v
    maps = []
    for r in range(8):
        mm = dict(m)
        mm["x"] = np.ascontiguousarray(X[r])
        maps.append(mm)
    return maps


_PROG = None

def _get_prog():
    global _PROG
    if _PROG is None:
        kb, env = build_program()
        _PROG = kb.realize()
    return _PROG


def kernel(**inputs):
    nc = _get_prog()
    maps = make_in_maps(inputs)
    res = run_bass_kernel_spmd(nc, maps, core_ids=list(range(8)))
    out = np.stack([np.asarray(res.results[r]["out"], dtype=np.float32)
                    for r in range(8)], axis=0)
    return out
